# version 27
# baseline (speedup 1.0000x reference)
import numpy as np
import ml_dtypes
import concourse.bass as bass
import concourse.mybir as mybir
from concourse.bass_utils import run_bass_kernel_spmd
from concourse.bass import IndirectOffsetOnAxis

F32 = mybir.dt.float32
BF16 = mybir.dt.bfloat16
AF = mybir.ActivationFunctionType
ALU = mybir.AluOpType
AX = mybir.AxisListType

D = 1024
HD = 256
NH = 4
CT = 4096
HT = 512
NG = 4
GT = 1024
FF = 2048
NE = 8
NFC = 16
NDC = 8
KR = 6
GUR = 20
DNR = 16
CAP = 1280
SPPC = CAP // 128
LN16 = float(np.log(16.0))
G_NG = NG
G_NE = NE
G_TILES = 8
G_STAGE = 9


def build_nc():
    nc = bass.Bass("TRN2", target_bir_lowering=False)

    x_in = nc.dram_tensor("x", [HT + CT, D], F32, kind="ExternalInput").ap()
    cs_in = nc.dram_tensor("cs", [HT + CT, 256], BF16, kind="ExternalInput").ap()
    mask_in = nc.dram_tensor("mask", [128, 640], F32, kind="ExternalInput").ap()
    clip_in = nc.dram_tensor("clip", [128, 4], F32, kind="ExternalInput").ap()
    qg_in = nc.dram_tensor("qg", [128, 256], F32, kind="ExternalInput").ap()
    kg_in = nc.dram_tensor("kg", [128, 256], F32, kind="ExternalInput").ap()
    idn_in = nc.dram_tensor("idn", [128, 128], F32, kind="ExternalInput").ap()
    rt_in = nc.dram_tensor("rt", [8, 128, 8], F32, kind="ExternalInput").ap()
    wq_in = nc.dram_tensor("wq", [2, 128, 4096], F32, kind="ExternalInput").ap()
    wkv_in = nc.dram_tensor("wkv", [128, 4096], F32, kind="ExternalInput").ap()
    wo_in = nc.dram_tensor("wo", [2, 128, 4096], F32, kind="ExternalInput").ap()
    gw_in = nc.dram_tensor("gw", [NE, 4, 128, 4096], F32, kind="ExternalInput").ap()
    uw_in = nc.dram_tensor("uw", [NE, 4, 128, 4096], F32, kind="ExternalInput").ap()
    dw_in = nc.dram_tensor("dw", [NE, 4, 128, 4096], F32, kind="ExternalInput").ap()
    gb_in = nc.dram_tensor("gb", [128, 128], F32, kind="ExternalInput").ap()
    ub_in = nc.dram_tensor("ub", [128, 128], F32, kind="ExternalInput").ap()
    db_in = nc.dram_tensor("db", [1, NE * D], F32, kind="ExternalInput").ap()
    ones_in = nc.dram_tensor("ones", [1, 128], F32, kind="ExternalInput").ap()
    cv_in = nc.dram_tensor("cv", [128, 3], F32, kind="ExternalInput").ap()
    tri_in = nc.dram_tensor("tri", [128, 128], F32, kind="ExternalInput").ap()
    dum_in = nc.dram_tensor("dum", [128, 32], mybir.dt.uint32, kind="ExternalInput").ap()
    tp_in = nc.dram_tensor("tp", [128, 64], mybir.dt.uint32, kind="ExternalInput").ap()
    out = nc.dram_tensor("out", [CT + 128, D], F32, kind="ExternalOutput").ap()
    xnd_in = nc.dram_tensor("xnd", [CT, D], BF16, kind="Internal").ap()
    wdr_in = nc.dram_tensor("wdr", [CT, 10], F32, kind="Internal").ap()
    gidxd_in = nc.dram_tensor("gidxd", [NE, CAP, 2], mybir.dt.uint32, kind="Internal").ap()

    A = nc.alloc_sbuf_tensor
    maskb = A("maskb", [128, 640], BF16)
    clipv = A("clipv", [128, 4], F32)
    cflt = A("cflt", [128, 3], F32)
    qg = A("qg_sb", [128, 256], BF16)
    kg = A("kg_sb", [128, 256], BF16)
    idf = A("idf", [128, 128], F32)
    rt = A("rt_sb", [128, 64], BF16)
    gbias = A("gb_sb", [128, 128], F32)
    ubias = A("ub_sb", [128, 128], F32)
    dbias = A("db_sb", [1, 2 * D], BF16)
    ones = A("ones_sb", [1, 128], BF16)
    xnTg = A("xnTg", [128, NDC * GT], BF16)
    yg = A("yg", [128, 2 * D], F32)
    ktr = A("ktr", [128, 2 * KR * 128], BF16)
    vr = A("vr", [128, KR * 256], BF16)
    css = A("css", [128, 2 * 256], BF16)
    xnf = A("xnf", [128, D], F32)
    scrA = A("scrA", [128, D], F32)
    ctxf = A("ctxf", [128, D], F32)
    rtmp = A("rtmp", [128, 4 * 128], F32)
    kn = A("kn", [128, 256], F32)
    qTs = [A("qTs0", [128, D], BF16), A("qTs1", [128, D], BF16)]
    smb = A("smb", [128, 640], F32)
    pexp = A("pexp", [128, 640], F32)
    pTb = A("pTb", [128, 640], BF16)
    wg = A("wg", [128, 4 * 64], F32)
    sml = A("sml", [128, 40], F32)
    gur = A("gur", [128, GUR * 1024], BF16)
    dnr = A("dnr", [128, DNR * 1024], BF16)
    hbuf = A("hbuf", [128, NFC * CAP], BF16)
    sgb = A("sgb", [128, 2 * 512], BF16)
    hbuf2 = A("hbuf2", [128, 8192], BF16)
    tri = A("tri_sb", [128, 128], BF16)
    cnt8 = A("cnt8", [128, 8], BF16)
    scr8 = A("scr8", [128, 100], F32)
    slotu = A("slotu", [128, 64], mybir.dt.uint32)
    gidx_all = A("gidx_all", [128, 8 * (2 * CAP // 128)], mybir.dt.uint32)
    wgath = A("wgath", [128, 2 * 2 * SPPC], F32)
    dummy_sb = A("dummy_sb", [128, 32], mybir.dt.uint32)
    tp_sb = A("tp_sb", [128, 64], mybir.dt.uint32)
    stg_sb = A("stg_sb", [128, 8 * 512], BF16)

    C_NM, C_SSQ, C_RSTD, C_KMS, C_KRS = 0, 1, 2, 3, 4
    C_QMS, C_QRS = 8, 12
    C_NMX, C_SUME, C_RCP = 16, 20, 24
    C_P8, C_T0, C_T1 = 28, 36, 38

    PB = [nc.alloc_psum_tensor(f"pb{i}", [128, 512], F32) for i in range(8)]

    ops = {e: [] for e in ["tensor", "vector", "scalar", "gpsimd", "sync"]}
    DMA_SEMS = (["DX%d" % i for i in range(4)] + ["DW%d" % i for i in range(4)]
                + ["DS%d" % i for i in range(4)] + ["DO", "DY"])
    cnt = {s: 0 for s in ["TE", "VE", "SE", "GE"] + DMA_SEMS}
    rot = {"DX": 0, "DW": 0, "DS": 0}

    def op(eng, fn, deps=(), inc=None, n=1):
        if inc in ("DX", "DW", "DS"):
            inc = "%s%d" % (inc, rot[inc] % 4)
            rot[inc[:2]] += 1
        dd = []
        for d in deps:
            if d is None:
                continue
            s, v = d
            if v > 0:
                dd.append((s, v))
        if inc in DMA_SEMS and cnt[inc] > 0:
            dd.append((inc, cnt[inc]))
        ops[eng].append((fn, tuple(dd), inc, n))
        if inc is not None:
            cnt[inc] += n
            return (inc, cnt[inc])
        return None

    readers = {}
    wrev = {}

    def rd(res, ev):
        if ev is not None:
            readers.setdefault(res, []).append(ev)

    def wdeps(res):
        d = list(readers.get(res, ()))
        readers[res] = []
        return d

    def peek(res):
        return list(readers.get(res, ()))

    def wr(res, ev):
        wrev[res] = ev

    def rdep(res):
        e = wrev.get(res)
        return [e] if e is not None else []

    def dma(eng, dst, src, inc, deps=()):
        return op(eng, lambda E, d=dst, s=src: E.dma_start(out=d, in_=s),
                  deps=deps, inc=inc, n=16)

    D_IDF = dma("sync", idf[:], idn_in[:, :], "DX")
    D_QG = dma("gpsimd", qg[:], qg_in[:, :], "DW")
    D_KG = dma("gpsimd", kg[:], kg_in[:, :], "DW")
    D_CLIP = dma("sync", clipv[:], clip_in[:, :], "DX")
    D_GB = dma("sync", gbias[:], gb_in[:, :], "DX")
    D_UB = dma("sync", ubias[:], ub_in[:, :], "DX")
    D_CV = dma("sync", cflt[:], cv_in[:, :], "DX")

    D_MASK = dma("gpsimd", maskb[:], mask_in[:, :], "DW")
    dma("gpsimd", ones[:], ones_in[:, :], "DW")
    D_RT = op("gpsimd", lambda E: E.dma_start(
        out=rt[:].rearrange("p (dc e) -> p dc e", e=8),
        in_=rt_in.rearrange("dc p e -> p dc e")), inc="DW", n=16)
    D_TRI = dma("gpsimd", tri[:], tri_in[:, :], "DW")
    REGS = {}

    def _mkregs(E):
        for name, val in [("b511", CAP - 1), ("bct", CT - 1),
                          ("bext", CT + 127)]:
            r = E.alloc_register(name)
            E.reg_mov(r, val)
            REGS[name] = r
        return None
    op("gpsimd", _mkregs)
    EV_DUM = dma("sync", dummy_sb[:], dum_in[:, :], "DX")
    EV_TP = dma("sync", tp_sb[:], tp_in[:, :], "DX")

    gu_pos = [0]
    dn_pos = [0]

    def gu_quad_dma(src4d, dst_from_col):
        p = gu_pos[0] % (GUR // 4)
        gu_pos[0] += 1
        base = p * 4 * 1024
        deps = []
        for i in range(4):
            deps += wdeps(("gur", p * 4 + i))
        dst = dst_from_col(base)
        ev = op("gpsimd", lambda E, d=dst, s=src4d: E.dma_start(out=d, in_=s),
                deps=deps, inc="DW", n=16)
        for i in range(4):
            wr(("gur", p * 4 + i), ev)
        return [base + i * 1024 for i in range(4)], p * 4, ev

    def dn_quad_dma(src2d):
        p = dn_pos[0] % (DNR // 4)
        dn_pos[0] += 1
        base = p * 4 * 1024
        deps = []
        for i in range(4):
            deps += wdeps(("dnr", p * 4 + i))
        dst = dnr[:, base:base + 4096]
        ev = op("gpsimd", lambda E, d=dst, sa=src2d: E.dma_start(out=d, in_=sa),
                deps=deps, inc="DW", n=16)
        for i in range(4):
            wr(("dnr", p * 4 + i), ev)
        return [base + i * 1024 for i in range(4)], ev

    tctr = [0]

    def tgroup(items, src_deps, dst_ap, dst_res_list, evac_extra_deps=(),
               eng="vector"):
        gidx = tctr[0]
        tctr[0] += 1
        bank = 6 + (gidx % 2)
        n = len(items)
        first = True
        tev = None
        for slot, s_ap in enumerate(items):
            pslice = PB[bank][:, slot * 128:(slot + 1) * 128]
            deps = (list(src_deps) + wdeps(("pbfree", bank)) + [D_IDF]) if first else list(src_deps)
            tev = op("tensor", lambda E, o=pslice, s=s_ap: E.transpose(o, s, idf[:]),
                     deps=deps, inc="TE" if slot == n - 1 else None)
            first = False
        wr(("pb", bank), tev)
        d1 = []
        for res in dst_res_list:
            if res is not None:
                d1 += wdeps(res)
        src_ap = PB[bank][:, 0:n * 128]
        if eng == "vector":
            ev = op("vector", lambda E, o=dst_ap, s=src_ap: E.tensor_copy(o, s),
                    deps=[tev] + d1 + list(evac_extra_deps), inc="VE")
        else:
            ev = op("scalar", lambda E, o=dst_ap, s=src_ap:
                    E.activation(o, s, AF.Copy),
                    deps=[tev] + d1 + list(evac_extra_deps), inc="SE")
        rd(("pbfree", bank), ev)
        for res in dst_res_list:
            if res is not None:
                wr(res, ev)
        return ev

    def mmgroup(banks, fns, deps=()):
        d = list(deps)
        for b in set(banks):
            d += wdeps(("pbfree", b))
        ev = None
        for i, fn in enumerate(fns):
            ev = op("tensor", fn, deps=d if i == 0 else (),
                    inc="TE" if i == len(fns) - 1 else None)
        for b in set(banks):
            wr(("pb", b), ev)
        return ev

    def bank_read(b, ev):
        rd(("pbfree", b), ev)

    n_out = [0]
    init_evs = []
    wdr_evs = []
    xnd_evs = []
    stages = [stg_sb[:, k * 512:(k + 1) * 512] for k in range(8)]
    STAGE_HOME = ["stgh%d" % k for k in range(8)]
    NST = len(stages)

    WQ = {}
    WKV = {}
    WO = {}
    wevs = []
    cols, _, _ev = gu_quad_dma(wq_in[0], lambda b: gur[:, b:b + 4096])
    for dc in range(4):
        WQ[dc] = cols[dc]
    wevs.append(_ev)
    cols, _, _ev = gu_quad_dma(wq_in[1], lambda b: gur[:, b:b + 4096])
    for dc in range(4):
        WQ[4 + dc] = cols[dc]
    wevs.append(_ev)
    cols, _, _ev = gu_quad_dma(wkv_in[:, :], lambda b: gur[:, b:b + 4096])
    for dc in range(8):
        WKV[dc] = cols[dc // 2] + (dc % 2) * 512
    wevs.append(_ev)
    cols, _, _ev = gu_quad_dma(wo_in[0], lambda b: gur[:, b:b + 4096])
    for dc in range(4):
        WO[dc] = cols[dc]
    wevs.append(_ev)
    cols, _, _ev = gu_quad_dma(wo_in[1], lambda b: gur[:, b:b + 4096])
    for dc in range(4):
        WO[4 + dc] = cols[dc]
    wevs.append(_ev)
    DW_W = list(wevs)

    for g in range(G_NG):
        def early(kind, kti, WQ, WKV, WO, DW_W, g):
            halo = kind == "halo"
            j = kti - 4
            tj = j % 8
            par = kti % 2
            if halo:
                xdst, xres = scrA[:, :], "scrA"
            else:
                xdst, xres = yg[:, (tj % 2) * D:(tj % 2 + 1) * D], ("yg", tj % 2)
            xdeps = wdeps(xres)
            xev = op("sync", lambda E, d=xdst, s=x_in[kti * 128:(kti + 1) * 128, :]:
                     E.dma_start(out=d, in_=s), deps=xdeps, inc="DX", n=16)
            wr(xres, xev)
            csl = kti % 2
            cdst = css[:, csl * 256:(csl + 1) * 256]
            cev = op("sync", lambda E, d=cdst, s=cs_in[kti * 128:(kti + 1) * 128, :]:
                     E.dma_start(out=d, in_=s), deps=wdeps(("css", csl)),
                     inc="DX", n=16)
            wr(("css", csl), cev)
            cosap = css[:, csl * 256: csl * 256 + 128]
            sinap = css[:, csl * 256 + 128: csl * 256 + 256]
            xap = xdst

            e1 = op("vector", lambda E, x=xap: E.reduce_sum(
                sml[:, C_NM:C_NM + 1], x, axis=AX.X, negate=True),
                deps=[xev] + wdeps("c_nm"), inc="VE")
            rd(xres, e1)
            e2 = op("vector", lambda E: E.tensor_scalar_mul(
                sml[:, C_NM:C_NM + 1], sml[:, C_NM:C_NM + 1], 1.0 / D),
                deps=[e1], inc="VE")
            wr("c_nm", e2)
            e3 = op("scalar", lambda E, x=xap: E.activation(
                xnf[:], x, AF.Square, bias=sml[:, C_NM:C_NM + 1],
                accum_out=sml[:, C_SSQ:C_SSQ + 1]),
                deps=[e2] + wdeps("xnf"), inc="SE")
            rd(xres, e3)
            rd("c_nm", e3)
            e4 = op("scalar", lambda E: E.activation(
                sml[:, C_SSQ:C_SSQ + 1], sml[:, C_SSQ:C_SSQ + 1], AF.Ln,
                scale=1.0 / D, bias=cflt[:, 0:1]), deps=[e3, D_CV], inc="SE")
            e5 = op("scalar", lambda E: E.activation(
                sml[:, C_RSTD:C_RSTD + 1], sml[:, C_SSQ:C_SSQ + 1], AF.Exp,
                scale=-0.5), deps=[e4] + wdeps("c_rstd"), inc="SE")
            wr("c_rstd", e5)
            e6 = op("vector", lambda E, x=xap: E.tensor_scalar(
                xnf[:], x, sml[:, C_NM:C_NM + 1], sml[:, C_RSTD:C_RSTD + 1],
                op0=ALU.add, op1=ALU.mult), deps=[e5] + wdeps("xnf"), inc="VE")
            rd(xres, e6)
            rd("c_nm", e6)
            rd("c_rstd", e6)
            wr("xnf", e6)
            if not halo:
                xnb = hbuf[:, 8192 + tj * 1024: 8192 + (tj + 1) * 1024]
                blkd = wdeps(("hb", 0, tj)) if tj < 4 else wdeps(("hb", 0, tj))
                cb_ev = op("vector", lambda E, d=xnb: E.tensor_copy(d, xnf[:]),
                           deps=[e6] + blkd, inc="VE")
                rd("xnf", cb_ev)
                xd_ev = op("sync", lambda E, s=xnb,
                           d=xnd_in[g * GT + tj * 128: g * GT + (tj + 1) * 128, :]:
                           E.dma_start(out=d, in_=s), deps=[cb_ev], inc="DX", n=16)
                rd(("hb", 0, tj), xd_ev)
                wr(("hb", 0, tj), cb_ev)
                xnd_evs.append(xd_ev)

            xnT_evs = []
            xnT_aps = []
            xnTg3 = xnTg[:].rearrange("p (dc t) -> p dc t", t=GT)
            for g4 in range(2):
                items = [xnf[:, dc * 128:(dc + 1) * 128]
                         for dc in range(g4 * 4, g4 * 4 + 4)]
                if halo:
                    dst = qTs[par][:, g4 * 512:(g4 + 1) * 512]
                    dres = [("qTs", par, dc) for dc in range(g4 * 4, g4 * 4 + 4)]
                else:
                    dst = xnTg3[:, g4 * 4:(g4 + 1) * 4, tj * 128:(tj + 1) * 128]
                    dres = [("xnTg", dc, tj) for dc in range(g4 * 4, g4 * 4 + 4)]
                ev = tgroup(items, [e6], dst, dres)
                rd("xnf", ev)
                xnT_evs.append(ev)
            for dc in range(NDC):
                if halo:
                    xnT_aps.append(qTs[par][:, dc * 128:(dc + 1) * 128])
                else:
                    xnT_aps.append(xnTg[:, dc * GT + tj * 128: dc * GT + (tj + 1) * 128])

            proj_deps = xnT_evs + DW_W

            fns = []
            for dc in range(NDC):
                fns.append(lambda E, l=xnT_aps[dc], r=gur[:, WKV[dc]:WKV[dc] + 256],
                           st=(dc == 0), sp=(dc == NDC - 1):
                           E.matmul(PB[2][:, 0:256], l, r, start=st, stop=sp))
            for dc in range(NDC):
                fns.append(lambda E, l=xnT_aps[dc],
                           r=gur[:, WKV[dc] + 256:WKV[dc] + 512],
                           st=(dc == 0), sp=(dc == NDC - 1):
                           E.matmul(PB[2][:, 256:512], l, r, start=st, stop=sp))
            kv_ev = mmgroup([2], fns, deps=proj_deps)
            for dc in range(NDC):
                rd(("gur", WKV[dc] // 1024), kv_ev)

            q_ev = None
            if not halo:
                fns = []
                for nc_ in range(2):
                    for dc in range(NDC):
                        fns.append(lambda E, b=nc_, l=xnT_aps[dc],
                                   r=gur[:, WQ[dc] + nc_ * 512:WQ[dc] + nc_ * 512 + 512],
                                   st=(dc == 0), sp=(dc == NDC - 1):
                                   E.matmul(PB[b][:, :], l, r, start=st, stop=sp))
                q_ev = mmgroup([0, 1], fns, deps=proj_deps)
                for dc in range(NDC):
                    rd(("gur", WQ[dc] // 1024), q_ev)
            e7 = op("scalar", lambda E: E.activation(
                xnf[:, 0:256], PB[2][:, 0:256], AF.Square,
                accum_out=sml[:, C_KMS:C_KMS + 1]),
                deps=[kv_ev], inc="SE")
            rd("xnf", e7)
            bank_read(2, e7)
            e8 = op("scalar", lambda E: E.activation(
                sml[:, C_KRS:C_KRS + 1], sml[:, C_KMS:C_KMS + 1], AF.Ln,
                scale=1.0 / HD, bias=cflt[:, 1:2]), deps=[e7, D_CV] + wdeps("c_krs"),
                inc="SE")
            e9 = op("scalar", lambda E: E.activation(
                sml[:, C_KRS:C_KRS + 1], sml[:, C_KRS:C_KRS + 1], AF.Exp,
                scale=-0.5), deps=[e8], inc="SE")
            wr("c_krs", e9)
            e10 = op("vector", lambda E: E.scalar_tensor_tensor(
                kn[:], PB[2][:, 0:256], sml[:, C_KRS:C_KRS + 1], kg[:],
                op0=ALU.mult, op1=ALU.mult),
                deps=[e9, kv_ev, D_KG] + wdeps("kn"), inc="VE")
            rd("c_krs", e10)
            bank_read(2, e10)
            r1 = op("vector", lambda E, c=cosap: E.tensor_mul(
                rtmp[:, 0:128], kn[:, 0:128], c),
                deps=[e10, cev] + wdeps("rt0"), inc="VE")
            rd(("css", csl), r1)
            r2 = op("vector", lambda E, s=sinap: E.tensor_mul(
                rtmp[:, 128:256], kn[:, 128:256], s), deps=[r1], inc="VE")
            rd(("css", csl), r2)
            r3 = op("vector", lambda E, s=sinap: E.tensor_mul(
                rtmp[:, 256:384], kn[:, 0:128], s), deps=[r2], inc="VE")
            rd(("css", csl), r3)
            r4 = op("vector", lambda E, c=cosap: E.tensor_mul(
                rtmp[:, 384:512], kn[:, 128:256], c), deps=[r3], inc="VE")
            rd(("css", csl), r4)
            r5 = op("vector", lambda E: E.tensor_sub(
                kn[:, 0:128], rtmp[:, 0:128], rtmp[:, 128:256]), deps=[r4], inc="VE")
            r6 = op("vector", lambda E: E.tensor_add(
                kn[:, 128:256], rtmp[:, 256:384], rtmp[:, 384:512]), deps=[r5], inc="VE")
            wr("rt0", r6)
            wr("kn", r6)
            rs = kti % KR
            vev = op("vector", lambda E, d=vr[:, rs * 256:(rs + 1) * 256]:
                     E.tensor_copy(d, PB[2][:, 256:512]),
                     deps=[kv_ev] + wdeps(("vr", rs)), inc="VE")
            wr(("vr", rs), vev)
            bank_read(2, vev)

            return dict(halo=halo, j=j, tj=tj, par=par, kti=kti, csl=csl,
                        cev=cev, kv_ev=kv_ev, q_ev=q_ev, cosap=cosap,
                        sinap=sinap, WO=WO, DW_W=DW_W, r6=r6)

        def earlyB(est):
            halo = est["halo"]
            j = est["j"]
            tj = est["tj"]
            par = est["par"]
            kti = est["kti"]
            csl = est["csl"]
            cev = est["cev"]
            kv_ev = est["kv_ev"]
            q_ev = est["q_ev"]
            cosap = est["cosap"]
            sinap = est["sinap"]
            rs = kti % KR
            ktr3 = ktr[:].rearrange("p (kc r) -> p kc r", r=KR * 128)
            ev = tgroup([kn[:, 0:128], kn[:, 128:256]], [est["r6"]],
                        ktr3[:, :, rs * 128:(rs + 1) * 128],
                        [("ktr", rs, 0), ("ktr", rs, 1)], eng="scalar")
            rd("kn", ev)
            if halo:
                return None
            eSl = []
            for h in range(NH):
                bank = h // 2
                colr = (h % 2) * 256
                eS = op("scalar", lambda E, b=bank, c=colr, h=h: E.activation(
                    xnf[:, h * 256:(h + 1) * 256], PB[b][:, c:c + 256], AF.Square,
                    accum_out=sml[:, C_QMS + h:C_QMS + h + 1]),
                    deps=[q_ev], inc="SE")
                rd("xnf", eS)
                bank_read(bank, eS)
                eSl.append(eS)
            eEl = []
            for h in range(NH):
                eL = op("scalar", lambda E, h=h: E.activation(
                    sml[:, C_QRS + h:C_QRS + h + 1],
                    sml[:, C_QMS + h:C_QMS + h + 1], AF.Ln,
                    scale=1.0 / HD, bias=cflt[:, 1:2]),
                    deps=[eSl[3], D_CV] + wdeps(("c_qrs", h)), inc="SE")
                eE = op("scalar", lambda E, h=h: E.activation(
                    sml[:, C_QRS + h:C_QRS + h + 1],
                    sml[:, C_QRS + h:C_QRS + h + 1], AF.Exp,
                    scale=-0.5, bias=cflt[:, 2:3]), deps=[eL], inc="SE")
                wr(("c_qrs", h), eE)
                eEl.append(eE)
            qstt = []
            for h in range(NH):
                bank = h // 2
                colr = (h % 2) * 256
                eT = op("vector", lambda E, b=bank, c=colr, h=h: E.scalar_tensor_tensor(
                    scrA[:, h * 256:(h + 1) * 256], PB[b][:, c:c + 256],
                    sml[:, C_QRS + h:C_QRS + h + 1], qg[:],
                    op0=ALU.mult, op1=ALU.mult),
                    deps=[eEl[3], q_ev, D_QG] + (wdeps("scrA") if h == 0 else []),
                    inc="VE")
                rd(("c_qrs", h), eT)
                bank_read(bank, eT)
                qstt.append(eT)
            qlast = None
            qe3 = scrA[:].rearrange("p (h d) -> p h d", d=256)[:, :, 0:128]
            qo3 = scrA[:].rearrange("p (h d) -> p h d", d=256)[:, :, 128:256]
            rt3 = rtmp[:].rearrange("p (h d) -> p h d", d=128)
            xf3a = xnf[:, 0:512].rearrange("p (h d) -> p h d", d=128)
            xf3b = xnf[:, 512:1024].rearrange("p (h d) -> p h d", d=128)
            cos3 = cosap.rearrange("p (o d) -> p o d", o=1).broadcast_to([128, 4, 128])
            sin3 = sinap.rearrange("p (o d) -> p o d", o=1).broadcast_to([128, 4, 128])
            t1 = op("vector", lambda E: E.tensor_mul(rt3, qe3, cos3),
                    deps=[qstt[3], cev] + wdeps("rt0"), inc="VE")
            rd(("css", csl), t1)
            t2 = op("vector", lambda E: E.tensor_mul(xf3a, qo3, sin3),
                    deps=[t1] + wdeps("xnf"), inc="VE")
            rd(("css", csl), t2)
            t3 = op("vector", lambda E: E.tensor_mul(xf3b, qe3, sin3),
                    deps=[t2], inc="VE")
            rd(("css", csl), t3)
            t4 = op("vector", lambda E: E.tensor_sub(qe3, rt3, xf3a),
                    deps=[t3], inc="VE")
            t5 = op("vector", lambda E: E.tensor_mul(rt3, qo3, cos3),
                    deps=[t4], inc="VE")
            rd(("css", csl), t5)
            qlast = op("vector", lambda E: E.tensor_add(qo3, xf3b, rt3),
                       deps=[t5], inc="VE")
            wr("rt0", qlast)
            wr("xnf", qlast)
            wr("scrA", qlast)
            qT_evs = []
            for g4 in range(2):
                items = [scrA[:, dc * 128:(dc + 1) * 128]
                         for dc in range(g4 * 4, g4 * 4 + 4)]
                dst = qTs[par][:, g4 * 512:(g4 + 1) * 512]
                dres = [("qTs", par, dc) for dc in range(g4 * 4, g4 * 4 + 4)]
                ev = tgroup(items, [qlast], dst, dres, eng="scalar")
                rd("scrA", ev)
                qT_evs.append(ev)
            return dict(j=j, tj=tj, par=par, qT_evs=qT_evs,
                        WO=est["WO"], DW_W=est["DW_W"])

        def lateA(st):
            j = st["j"]
            tj = st["tj"]
            par = st["par"]
            qT_evs = st["qT_evs"]
            WO = st["WO"]
            DW_W = st["DW_W"]
            ctx_done = []
            hstate = {}

            def sc_blk(h):
                sa = 3 if h % 2 == 0 else 2
                fns = []
                sdeps = [qT_evs[h // 2]]
                for b in range(5):
                    krs = (j + b) % KR
                    sdeps += rdep(("ktr", krs, 0)) + rdep(("ktr", krs, 1))
                    for kc in range(2):
                        if b < 4:
                            dstp = PB[sa][:, b * 128:(b + 1) * 128]
                        else:
                            dstp = PB[4][:, h * 128:(h + 1) * 128]
                        fns.append(lambda E, d=dstp,
                                   l=qTs[par][:, (2 * h + kc) * 128:(2 * h + kc + 1) * 128],
                                   r=ktr[:, kc * KR * 128 + krs * 128:
                                         kc * KR * 128 + (krs + 1) * 128],
                                   st=(kc == 0), sp=(kc == 1):
                                   E.matmul(d, l, r, start=st, stop=sp))
                s_ev = mmgroup([sa, 4], fns, deps=sdeps)
                for b in range(5):
                    rd(("ktr", (j + b) % KR, 0), s_ev)
                    rd(("ktr", (j + b) % KR, 1), s_ev)
                rd(("qTs", par, 2 * h), s_ev)
                rd(("qTs", par, 2 * h + 1), s_ev)
                m1e = op("vector", lambda E, sa=sa: E.tensor_add(
                    smb[:, 0:512], PB[sa][:, :], maskb[:, 0:512]),
                    deps=[s_ev, D_MASK] + wdeps("smb"), inc="VE")
                bank_read(sa, m1e)
                m2e = op("vector", lambda E, h=h: E.tensor_add(
                    smb[:, 512:640], PB[4][:, h * 128:(h + 1) * 128],
                    maskb[:, 512:640]), deps=[s_ev], inc="VE")
                bank_read(4, m2e)
                smdone = m2e
                if j < 4:
                    wclip = 512 - 128 * j
                    smdone = op("vector", lambda E, w=wclip, jj=j: E.tensor_scalar_add(
                        smb[:, 0:w], smb[:, 0:w], clipv[:, jj:jj + 1]),
                        deps=[m2e, D_CLIP], inc="VE")
                wr("smb", smdone)
                nx = op("vector", lambda E, h=h: E.reduce_max(
                    sml[:, C_NMX + h:C_NMX + h + 1], smb[:, :], axis=AX.X,
                    negate=True), deps=[smdone] + wdeps(("c_nmx", h)), inc="VE")
                rd("smb", nx)
                hstate[h] = nx

            def ex_blk(h):
                nx = hstate[h]
                ex = op("scalar", lambda E, h=h: E.activation(
                    pexp[:, :], smb[:, :], AF.Exp,
                    bias=sml[:, C_NMX + h:C_NMX + h + 1],
                    accum_out=sml[:, C_SUME + h:C_SUME + h + 1]),
                    deps=[nx] + wdeps("pexp") + wdeps(("c_sume", h)), inc="SE")
                rd("smb", ex)
                rd(("c_nmx", h), ex)
                wr("pexp", ex)
                wr(("c_sume", h), ex)
                rch = op("vector", lambda E, h=h: E.reciprocal(
                    sml[:, C_RCP + h:C_RCP + h + 1],
                    sml[:, C_SUME + h:C_SUME + h + 1]), deps=[ex], inc="VE")
                rd(("c_sume", h), rch)
                hstate[h] = (ex, rch)

            def pt_blk(h):
                ex, rch = hstate[h]
                ev1 = tgroup([pexp[:, b * 128:(b + 1) * 128] for b in range(4)],
                             [ex], pTb[:, 0:512],
                             [("pTb", b) for b in range(4)])
                rd("pexp", ev1)
                ev2 = tgroup([pexp[:, 512:640]], [ex], pTb[:, 512:640],
                             [("pTb", 4)])
                rd("pexp", ev2)
                pT_evs = [ev1, ev2]
                cc = (h % 2) * 256
                fns = []
                avdeps = [ev1, ev2]
                for b in range(5):
                    krs = (j + b) % KR
                    avdeps += rdep(("vr", krs))
                    fns.append(lambda E, bb=b, d=PB[5][:, cc:cc + 256],
                               l=pTb[:, b * 128:(b + 1) * 128],
                               r=vr[:, krs * 256:(krs + 1) * 256]:
                               E.matmul(d, l, r, start=(bb == 0), stop=(bb == 4)))
                av_ev = mmgroup([5], fns, deps=avdeps)
                for b in range(5):
                    rd(("vr", (j + b) % KR), av_ev)
                    rd(("pTb", b), av_ev)
                cdone = op("vector", lambda E, h=h, cc=cc: E.tensor_scalar_mul(
                    ctxf[:, h * 256:(h + 1) * 256], PB[5][:, cc:cc + 256],
                    sml[:, C_RCP + h:C_RCP + h + 1]),
                    deps=[av_ev, rch] + rdep(("pb", 5)) +
                         (wdeps("ctxf") if h == 0 else []), inc="VE")
                bank_read(5, cdone)
                ctx_done.append(cdone)

            sc_blk(0)
            ex_blk(0)
            sc_blk(1)
            pt_blk(0)
            ex_blk(1)
            sc_blk(2)
            pt_blk(1)
            ex_blk(2)
            sc_blk(3)
            pt_blk(2)
            ex_blk(3)
            pt_blk(3)
            wr("ctxf", ctx_done[-1])
            return dict(j=j, tj=tj, ctx_done=ctx_done, WO=WO, DW_W=DW_W)

        def lateB(lst):
            j = lst["j"]
            tj = lst["tj"]
            ctx_done = lst["ctx_done"]
            WO = lst["WO"]
            DW_W = lst["DW_W"]

            ao_ev = None
            for rnd in range(2):
                ev = tgroup([ctxf[:, (rnd * 4 + i) * 128:(rnd * 4 + i + 1) * 128]
                             for i in range(4)],
                            [ctx_done[-1]], pTb[:, 0:512],
                            [("pTb", i) for i in range(4)])
                rd("ctxf", ev)
                cT_evs = [ev]
                fns = []
                for nc_ in range(2):
                    bank = 3 if nc_ == 0 else 4
                    for i in range(4):
                        dc = rnd * 4 + i
                        fns.append(lambda E, b=bank, l=pTb[:, i * 128:(i + 1) * 128],
                                   r=gur[:, WO[dc] + nc_ * 512:WO[dc] + nc_ * 512 + 512],
                                   st=(dc == 0), sp=(dc == NDC - 1):
                                   E.matmul(PB[b][:, :], l, r, start=st, stop=sp))
                if rnd == 0:
                    ao_ev = mmgroup([3, 4], fns, deps=cT_evs + DW_W)
                else:
                    for i, fn in enumerate(fns):
                        ao_ev = op("tensor", fn, deps=cT_evs if i == 0 else (),
                                   inc="TE" if i == len(fns) - 1 else None)
                    wr(("pb", 3), ao_ev)
                    wr(("pb", 4), ao_ev)
                for i in range(4):
                    rd(("pTb", i), ao_ev)
            for dc in range(NDC):
                rd(("gur", WO[dc] // 1024), ao_ev)
            ys = (tj % 2) * D
            y1 = op("vector", lambda E, ys=ys: E.tensor_add(
                yg[:, ys:ys + 512], PB[3][:, :], yg[:, ys:ys + 512]),
                deps=[ao_ev], inc="VE")
            bank_read(3, y1)
            rd(("yg", tj % 2), y1)
            y2 = op("vector", lambda E, ys=ys: E.tensor_add(
                yg[:, ys + 512:ys + D], PB[4][:, :],
                yg[:, ys + 512:ys + D]), deps=[ao_ev], inc="VE")
            bank_read(4, y2)
            rd(("yg", tj % 2), y2)
            wr(("yginit", tj), y2)

            fns = []
            for dc in range(NDC):
                fns.append(lambda E,
                           l=xnTg[:, dc * GT + tj * 128: dc * GT + (tj + 1) * 128],
                           r=rt[:, dc * 8:(dc + 1) * 8],
                           st=(dc == 0), sp=(dc == NDC - 1):
                           E.matmul(PB[3][:, 0:8], l, r, start=st, stop=sp))
            rts_d = [wrev[("xnTg", dc, tj)] for dc in range(NDC)] + [D_RT]
            r_ev = mmgroup([3], fns, deps=rts_d)
            for dc in range(NDC):
                rd(("xnTg", dc, tj), r_ev)
            P8 = sml[:, C_P8:C_P8 + 8]
            T0 = sml[:, C_T0:C_T0 + 1]
            T1 = sml[:, C_T1:C_T1 + 1]
            b1 = op("vector", lambda E: E.reduce_max(
                T0, PB[3][:, 0:8], axis=AX.X, negate=True),
                deps=[r_ev] + wdeps("rtT0"), inc="VE")
            bank_read(3, b1)
            b2 = op("scalar", lambda E: E.activation(
                P8, PB[3][:, 0:8], AF.Exp, bias=T0, accum_out=T1),
                deps=[b1] + wdeps("rtP8"), inc="SE")
            bank_read(3, b2)
            rd("rtT0", b2)
            b3 = op("vector", lambda E: E.reciprocal(T1, T1), deps=[b2], inc="VE")
            b4 = op("vector", lambda E: E.tensor_scalar_mul(P8, P8, T1),
                    deps=[b3], inc="VE")
            wr("rtP8", b4)
            b5 = op("vector", lambda E: E.reduce_max(T0, P8, axis=AX.X),
                    deps=[b4], inc="VE")
            b6 = op("vector", lambda E: E.tensor_scalar(
                pexp[:, 0:8], P8, T0, None, op0=ALU.is_ge),
                deps=[b5] + wdeps("pexp"), inc="VE")
            b7 = op("vector", lambda E: E.scalar_tensor_tensor(
                pexp[:, 8:16], pexp[:, 0:8], -1e30, P8,
                op0=ALU.mult, op1=ALU.add), deps=[b6], inc="VE")
            b8 = op("vector", lambda E: E.reduce_max(
                T0, pexp[:, 8:16], axis=AX.X), deps=[b7], inc="VE")
            b9 = op("vector", lambda E: E.tensor_scalar(
                pexp[:, 0:8], P8, T0, None, op0=ALU.is_ge), deps=[b8], inc="VE")
            b10 = op("vector", lambda E: E.tensor_mul(
                pexp[:, 0:8], pexp[:, 0:8], P8), deps=[b9], inc="VE")
            b11 = op("vector", lambda E: E.reduce_sum(
                T0, pexp[:, 0:8], axis=AX.X), deps=[b10], inc="VE")
            b12 = op("vector", lambda E: E.tensor_scalar_add(T0, T0, 1e-9),
                     deps=[b11], inc="VE")
            b13 = op("vector", lambda E: E.reciprocal(T0, T0), deps=[b12], inc="VE")
            wr("rtT0", b13)
            b14 = op("vector", lambda E, c=g * 64 + tj * 8: E.tensor_scalar_mul(
                wg[:, c:c + 8], pexp[:, 0:8], T0),
                deps=[b13] + wdeps(("wg", g * 8 + tj)), inc="VE")
            wr(("wg", g * 8 + tj), b14)
            rd("rtP8", b14)
            rd("pexp", b14)
            oi = op("sync", lambda E, ys=(tj % 2) * D, tj=tj,
                    d=out[g * GT + tj * 128: g * GT + (tj + 1) * 128, :]:
                    E.dma_start(out=d, in_=yg[:, ys:ys + D]),
                    deps=[y2], inc="DO", n=16)
            rd(("yg", tj % 2), oi)
            wr(("yginit", tj), oi)
            init_evs.append(oi)
            n_out[0] += 1
            wv_ev = op("sync", lambda E, c=g * 64 + tj * 8,
                       d=wdr_in[g * GT + tj * 128: g * GT + (tj + 1) * 128, 0:8]:
                       E.dma_start(out=d, in_=wg[:, c:c + 8]),
                       deps=[b14], inc="DX", n=16)
            rd(("wg", g * 8 + tj), wv_ev)
            wdr_evs.append(wv_ev)

        tiles = ([("halo", k) for k in range(4)] if g == 0 else []) + \
                [("own", g * 8 + j + 4) for j in range(G_TILES)]
        pend_sA = None
        pend_sB = None
        for kind, kti in tiles:
            est = early(kind, kti, WQ, WKV, WO, DW_W, g)
            if pend_sA is not None:
                pend_sB = lateA(pend_sA)
                pend_sA = None
            st = earlyB(est)
            if pend_sB is not None:
                lateB(pend_sB)
                pend_sB = None
            pend_sA = st
        if pend_sA is not None:
            lateB(lateA(pend_sA))

    NCOL = G_NG * 8
    SPP = CAP // 128
    IW = 2 * SPP
    wg3 = wg[:].rearrange("p (c e) -> p c e", e=8)
    wgdeps = [wrev[("wg", i)] for i in range(NCOL) if ("wg", i) in wrev]
    A32 = scr8[:, 0:NCOL]
    B32 = scr8[:, 32:32 + NCOL]
    C32 = scr8[:, 64:64 + NCOL]
    CNT1 = scr8[:, 96:97]
    def build_index(e):
        i1 = op("vector", lambda E, e=e: E.tensor_scalar(
            A32.rearrange("p (a b) -> p a b", b=1), wg3[:, 0:NCOL, e:e + 1], 0.0,
            None, op0=ALU.is_gt), deps=wgdeps + wdeps("scr8"), inc="VE")
        i2 = op("vector", lambda E: E.reduce_sum(
            CNT1, A32, axis=AX.X), deps=[i1], inc="VE")
        i3 = op("vector", lambda E: E.tensor_copy(
            cnt8[:].rearrange("p (a b) -> p a b", b=1),
            CNT1.rearrange("p (a b) -> p a b", b=1)
            .broadcast_to([128, 8, 1])), deps=[i2] + wdeps("cnt8"), inc="VE")
        wr("cnt8", i3)
        pse = mmgroup([0], [lambda E, e=e: E.matmul(
            PB[0][:, e * 8:(e + 1) * 8], tri[:], cnt8[:],
            start=True, stop=True)], deps=[i3, D_TRI])
        rd("cnt8", pse)
        cur, nxt = A32, B32
        lvl = 1
        sp_ = i1
        while lvl < NCOL:
            e1 = op("vector", lambda E, c=cur, n=nxt, l=lvl:
                    E.tensor_copy(n[:, 0:l], c[:, 0:l]), deps=[sp_], inc="VE")
            sp_ = op("vector", lambda E, c=cur, n=nxt, l=lvl:
                     E.tensor_add(n[:, l:NCOL], c[:, l:NCOL], c[:, 0:NCOL - l]),
                     deps=[sp_, e1], inc="VE")
            cur, nxt = nxt, (C32 if nxt is B32 else B32)
            lvl *= 2
        if cur is not B32:
            sp_ = op("vector", lambda E: E.tensor_copy(B32, C32),
                     deps=[sp_], inc="VE")
        sp_ = op("vector", lambda E: E.scalar_tensor_tensor(
            B32, B32, 1.0, A32, op0=ALU.mult, op1=ALU.subtract),
            deps=[sp_], inc="VE")
        sp_ = op("vector", lambda E, e=e: E.tensor_scalar(
            B32, B32, PB[0][:, e * 8:e * 8 + 1], None, op0=ALU.add),
            deps=[sp_, pse], inc="VE")
        bank_read(0, sp_)
        sp_ = op("vector", lambda E: E.tensor_scalar_add(C32, B32, -60000.0),
                 deps=[sp_], inc="VE")
        sp_ = op("vector", lambda E: E.tensor_mul(C32, C32, A32),
                 deps=[sp_], inc="VE")
        sp_ = op("vector", lambda E: E.tensor_scalar_add(C32, C32, 60000.0),
                 deps=[sp_], inc="VE")
        sb = (e % 2) * 32
        s10 = op("vector", lambda E, sb=sb: E.tensor_copy(
            slotu[:, sb:sb + NCOL], C32),
            deps=[sp_] + wdeps(("slotu", e % 2)), inc="VE")
        wr(("slotu", e % 2), s10)
        wr("scr8", s10)
        pf = op("sync", lambda E, e=e: E.dma_start(
            out=gidxd_in[e, 0:CAP].rearrange("(p s) j -> p (s j)", s=SPP),
            in_=dummy_sb[:, 0:IW]),
            deps=[EV_DUM] + wdeps(("gidxd", e)), inc="DX", n=16)
        scevs = []
        for c32 in range(NCOL):
            g_, tj = divmod(c32, 8)
            sc = op("gpsimd", lambda E, e=e,
                    ixs=slotu[:, sb + c32:sb + c32 + 1],
                    src_ap=tp_sb[:, g_ * 16 + tj * 2: g_ * 16 + tj * 2 + 2]:
                    E.indirect_dma_start(
                out=gidxd_in.rearrange("e s j -> (e s) j"),
                out_offset=IndirectOffsetOnAxis(ixs, 0),
                in_=src_ap,
                in_offset=None, element_offset=e * 2 * CAP,
                bounds_check=REGS["b511"], oob_is_err=False),
                deps=[pf, s10, EV_TP], inc="DW", n=16)
            rd(("slotu", e % 2), sc)
            scevs.append(sc)
        gl = op("sync", lambda E, e=e: E.dma_start(
            out=gidx_all[:, e * IW:(e + 1) * IW],
            in_=gidxd_in[e, 0:CAP].rearrange("(p s) j -> p (s j)", s=SPP)),
            deps=scevs, inc="DX", n=16)
        wr(("gidx", e), gl)
        for sc in scevs:
            rd(("gidxd", e), sc)
        rd(("gidxd", e), gl)


    for e in range(min(2, G_NE)):
        build_index(e)

    CHW = [512, 512, 256]

    def xgt_rhs(c01, dc, w):
        if c01 < 2:
            base = c01 * 4096 + dc * 512
            return hbuf2[:, base:base + w]
        return qTs[dc // 4][:, (dc % 4) * 256:(dc % 4) * 256 + w]

    def emit_gather(e, s8s, tevs_c):
        for s8 in s8s:
            c01 = s8 // 4 if s8 < 8 else 2
            sub = s8 % 4 if s8 < 8 else s8 - 8
            slot = s8 % 8
            icol = e * IW + s8 * 2
            wgo = (e % 2) * 2 * SPPC
            gw_ev = op("gpsimd", lambda E, s8=s8, wgo=wgo,
                       ix=gidx_all[:, icol:icol + 1]: E.indirect_dma_start(
                out=wgath[:, wgo + s8 * 2:wgo + (s8 + 1) * 2],
                out_offset=None,
                in_=wdr_in,
                in_offset=IndirectOffsetOnAxis(ix, 0),
                element_offset=e, bounds_check=REGS["bct"], oob_is_err=False),
                deps=rdep(("gidx", e)) + wdeps(("wgath", e % 2, s8)) + wdr_evs,
                inc="DW", n=16)
            rd(("gidx", e), gw_ev)
            wr(("wgath", e % 2, s8), gw_ev)
            xg_dst = xnTg[:, slot * 1024:(slot + 1) * 1024]
            xgdeps = wdeps(("xg", slot))
            for tj in range(8):
                xgdeps += wdeps(("xnTg", slot, tj))
            gx_ev = op("gpsimd", lambda E, d=xg_dst,
                       ix=gidx_all[:, icol:icol + 1]: E.indirect_dma_start(
                out=d, out_offset=None, in_=xnd_in,
                in_offset=IndirectOffsetOnAxis(ix, 0),
                bounds_check=REGS["bct"], oob_is_err=False),
                deps=rdep(("gidx", e)) + xgdeps + xnd_evs,
                inc="DW", n=16)
            rd(("gidx", e), gx_ev)
            wr(("xg", slot), gx_ev)
            for dc in range(NDC):
                if c01 < 2:
                    base = c01 * 4096 + dc * 512 + sub * 128
                    t_out = hbuf2[:, base:base + 128]
                else:
                    base = (dc % 4) * 256 + sub * 128
                    t_out = qTs[dc // 4][:, base:base + 128]
                t_in = xnTg[:, slot * 1024 + dc * 128:
                            slot * 1024 + (dc + 1) * 128]
                tdeps = [gx_ev] + wdeps(("xgT", c01))
                if c01 == 2:
                    tdeps += wdeps(("qTs", dc // 4, (dc % 4) * 2 + sub))
                tev = op("sync", lambda E, o=t_out, i=t_in:
                         E.dma_start_transpose(out=o, in_=i),
                         deps=tdeps, inc="DX", n=16)
                rd(("xg", slot), tev)
                wr(("xgT", c01), tev)
                if c01 == 2:
                    wr(("qTs", dc // 4, (dc % 4) * 2 + sub), tev)
                tevs_c[c01].append(tev)

    def gu_pair(e, q4, gcols, ucols, gdw):
        cols, _, gqe = gu_quad_dma(gw_in[e, q4], lambda b: gur[:, b:b + 4096])
        for i in range(4):
            gcols[q4 * 4 + i] = cols[i]
        cols, _, uqe = gu_quad_dma(uw_in[e, q4], lambda b: gur[:, b:b + 4096])
        for i in range(4):
            ucols[q4 * 4 + i] = cols[i]
            gdw[q4 * 4 + i] = [gqe, uqe]

    pend_t = {}
    if G_NE > 0:
        pend_t[0] = {0: [], 1: [], 2: []}
        emit_gather(0, range(SPP), pend_t[0])
    pend_gu = {}
    for e in range(G_NE):
        dcols = {}
        dnevs = []
        for q4 in range(4):
            cols, dqe = dn_quad_dma(dw_in[e, q4])
            for i in range(4):
                dcols[q4 * 4 + i] = cols[i]
            dnevs.append(dqe)

        dsl = (e % 2) * D
        db_ev = op("gpsimd", lambda E, d=dbias[0:1, dsl:dsl + D],
                   s=db_in[0:1, e * D:(e + 1) * D]: E.dma_start(out=d, in_=s),
                   deps=wdeps(("dbs", e % 2)), inc="DW", n=16)
        wr(("dbs", e % 2), db_ev)

        if e in pend_gu:
            gcols, ucols, gdw = pend_gu.pop(e)
        else:
            gcols, ucols, gdw = {}, {}, {}
            gu_pair(e, 0, gcols, ucols, gdw)
        tevs_c = pend_t.pop(e)

        for u in range(NFC * 3):
            fc, c01 = divmod(u, 3)
            if u in (3, 15, 27):
                gu_pair(e, u // 12 + 1, gcols, ucols, gdw)
            if u == 24 and e + 2 < G_NE:
                build_index(e + 2)
            if u == 39 and e + 1 < G_NE:
                ngc, nuc, ngd = {}, {}, {}
                gu_pair(e + 1, 0, ngc, nuc, ngd)
                pend_gu[e + 1] = (ngc, nuc, ngd)
            W = CHW[c01]
            par = u % 2
            gbk, ubk = 4 + 2 * par, 5 + 2 * par
            fns = []
            for dc in range(NDC):
                fns.append(lambda E, b=gbk, W=W,
                           l=gur[:, gcols[fc] + dc * 128:gcols[fc] + (dc + 1) * 128],
                           r=xgt_rhs(c01, dc, W),
                           st=(dc == 0), sp=(dc == NDC - 1):
                           E.matmul(PB[b][:, 0:W], l, r, start=st, stop=sp))
            for dc in range(NDC):
                fns.append(lambda E, b=ubk, W=W,
                           l=gur[:, ucols[fc] + dc * 128:ucols[fc] + (dc + 1) * 128],
                           r=xgt_rhs(c01, dc, W),
                           st=(dc == 0), sp=(dc == NDC - 1):
                           E.matmul(PB[b][:, 0:W], l, r, start=st, stop=sp))
            gudeps = list(gdw[fc]) + (tevs_c[c01] if u < 3 else [])
            gu_ev = mmgroup([gbk, ubk], fns, deps=gudeps)
            rd(("gur", gcols[fc] // 1024), gu_ev)
            rd(("gur", ucols[fc] // 1024), gu_ev)
            rd(("xgT", c01), gu_ev)
            if c01 == 2:
                for dc in range(NDC):
                    for s in range(2):
                        rd(("qTs", dc // 4, (dc % 4) * 2 + s), gu_ev)
            sl = u % 2
            sev = op("scalar", lambda E, b=gbk, col=e * 16 + fc, s=sl, W=W:
                     E.activation(sgb[:, s * 512:s * 512 + W], PB[b][:, 0:W],
                                  AF.Silu, bias=gbias[:, col:col + 1]),
                     deps=[gu_ev, D_GB] + wdeps(("sgb", sl)), inc="SE")
            bank_read(gbk, sev)
            wr(("sgb", sl), sev)
            hev = op("vector", lambda E, b=ubk, col=e * 16 + fc, s=sl, W=W,
                     d=hbuf[:, fc * 1280 + c01 * 512: fc * 1280 + c01 * 512 + W]:
                     E.scalar_tensor_tensor(
                         d, PB[b][:, 0:W], ubias[:, col:col + 1],
                         sgb[:, s * 512:s * 512 + W],
                         op0=ALU.add, op1=ALU.mult),
                     deps=[sev, gu_ev, D_UB] + wdeps(("h", fc, c01)), inc="VE")
            bank_read(ubk, hev)
            rd(("sgb", sl), hev)
            wr(("h", fc, c01), hev)

        gchunks = [[2, 3], [4, 5], [6, 7], [8, 9], [], []]
        if e + 1 < G_NE:
            pend_t[e + 1] = {0: [], 1: [], 2: []}
            emit_gather(e + 1, [0, 1], pend_t[e + 1])
        else:
            gchunks = [[]] * 6

        SUBG = [(0, 1, 2, 3), (4, 5, 6, 7), (8, 9)]
        stg = 0
        gci = 0
        for dh in range(2):
            for sg in SUBG:
                fns = []
                sweep_deps = list(dnevs) + [db_ev]
                for fc in range(NFC):
                    for sub in sg:
                        sweep_deps += rdep(("h", fc, sub // 4 if sub < 8 else 2))
                for fc in range(NFC):
                    for i, sub in enumerate(sg):
                        fns.append(lambda E, t=i,
                                   l=hbuf[:, fc * 1280 + sub * 128:
                                          fc * 1280 + (sub + 1) * 128],
                                   r=dnr[:, dcols[fc] + dh * 512:
                                         dcols[fc] + dh * 512 + 512],
                                   st=(fc == 0):
                                   E.matmul(PB[t][:, :], l, r,
                                            start=st, stop=False))
                for i, sub in enumerate(sg):
                    fns.append(lambda E, t=i,
                               r=dbias[0:1, dsl + dh * 512:
                                       dsl + dh * 512 + 512]:
                               E.matmul(PB[t][:, :], ones[0:1, :], r,
                                        start=False, stop=True))
                dn_ev = mmgroup(list(range(len(sg))), fns, deps=sweep_deps)
                rd(("dbs", e % 2), dn_ev)
                for fc in range(NFC):
                    for sub in sg:
                        rd(("h", fc, sub // 4 if sub < 8 else 2), dn_ev)
                    rd(("dnr", dcols[fc] // 1024), dn_ev)
                for i, sub in enumerate(sg):
                    k = stg % NST
                    sbuf_st = stages[k]
                    stg += 1
                    stage_home = STAGE_HOME[k]
                    wgo = (e % 2) * 2 * SPPC
                    eev = op("vector", lambda E, t=i, s=sbuf_st, sub=sub, wgo=wgo:
                             E.tensor_scalar_mul(
                                 s, PB[t][:, :],
                                 wgath[:, wgo + sub * 2: wgo + sub * 2 + 1]),
                             deps=[dn_ev] + rdep(("wgath", e % 2, sub)) +
                                  wdeps(("stage", k)) + wdeps(stage_home),
                             inc="VE")
                    bank_read(i, eev)
                    rd(("wgath", e % 2, sub), eev)
                    wr(("stage", k), eev)
                    wr(stage_home, eev)
                    sicol = e * IW + sub * 2 + 1
                    ysc = op("gpsimd", lambda E, dh=dh, s=sbuf_st, sicol=sicol,
                             ix=gidx_all[:, sicol:sicol + 1]:
                             E.indirect_dma_start(
                                 out=out,
                                 out_offset=IndirectOffsetOnAxis(ix, 0),
                                 in_=s, in_offset=None,
                                 element_offset=dh * 512,
                                 bounds_check=REGS["bext"], oob_is_err=True,
                                 compute_op=ALU.add),
                             deps=[eev] + rdep(("gidx", e)) + init_evs,
                             inc="DY", n=16)
                    rd(("gidx", e), ysc)
                    rd(("stage", k), ysc)
                    rd(stage_home, ysc)
                if e + 1 < G_NE and gchunks[gci]:
                    emit_gather(e + 1, gchunks[gci], pend_t[e + 1])
                gci += 1

    op("scalar", lambda E: E.nop(),
       deps=[("DO", 16 * n_out[0]), ("DY", cnt["DY"])])

    from contextlib import ExitStack
    with ExitStack() as _stk:
        sems = {}
        for s in ["TE", "VE", "SE", "GE"] + DMA_SEMS:
            sems[s] = _stk.enter_context(nc.semaphore("sem_" + s))
        block = _stk.enter_context(nc.Block())

        def emitter(name):
            def go(E):
                waited = {}
                for fn, deps, inc, n in ops[name]:
                    for s, v in deps:
                        if v > waited.get(s, 0):
                            E.wait_ge(sems[s], v)
                            waited[s] = v
                    instr = fn(E)
                    if inc is not None:
                        instr.then_inc(sems[inc], n)
            return go

        block.tensor(emitter("tensor"))
        block.vector(emitter("vector"))
        block.scalar(emitter("scalar"))
        block.gpsimd(emitter("gpsimd"))
        block.sync(emitter("sync"))
    return nc


def _host_prep(inputs):
    x = np.asarray(inputs["L0_attn_x"])
    Wq = np.asarray(inputs["L0_attn_Wq"])
    Wk = np.asarray(inputs["L0_attn_Wk"])
    Wv = np.asarray(inputs["L0_attn_Wv"])
    Wo = np.asarray(inputs["L0_attn_Wo"])
    qgam = np.asarray(inputs["L0_attn_q_gamma"])
    kgam = np.asarray(inputs["L0_attn_k_gamma"])
    rw = np.asarray(inputs["L0_ffn_router_w"])
    gwt = np.asarray(inputs["L0_ffn_gate_w"])
    gbt = np.asarray(inputs["L0_ffn_gate_b"])
    uwt = np.asarray(inputs["L0_ffn_up_w"])
    ubt = np.asarray(inputs["L0_ffn_up_b"])
    dwt = np.asarray(inputs["L0_ffn_down_w"])
    dbt = np.asarray(inputs["L0_ffn_down_b"])

    perm = np.concatenate([np.arange(0, 256, 2), np.arange(1, 256, 2)])
    wq_host = np.ascontiguousarray(
        Wq.reshape(4, 256, 1024)[:, perm, :].reshape(1024, 1024).T
        .reshape(2, 4, 128, 1024).transpose(0, 2, 1, 3).reshape(2, 128, 4096))
    WkT = Wk[perm, :].T
    WvT = Wv.T
    wkv_host = np.ascontiguousarray(np.concatenate(
        [WkT.reshape(8, 128, 256), WvT.reshape(8, 128, 256)], axis=2)
        .transpose(1, 0, 2).reshape(128, 4096))
    wo_host = np.ascontiguousarray(
        Wo.T.reshape(2, 4, 128, 1024).transpose(0, 2, 1, 3).reshape(2, 128, 4096))
    rt_host = np.ascontiguousarray(rw.T.reshape(8, 128, 8))
    qg_host = np.ascontiguousarray(np.broadcast_to(qgam[perm], (128, 256))).astype(np.float32)
    kg_host = np.ascontiguousarray(np.broadcast_to(kgam[perm], (128, 256))).astype(np.float32)

    inv = (1.0 / (np.float32(1e6) ** (np.arange(0, 256, 2, dtype=np.float32) /
                                      np.float32(256)))).astype(np.float32)
    qi = np.arange(128)[:, None]
    jc = np.arange(640)[None, :]
    band = np.where((jc >= qi) & (jc <= qi + 512), 0.0, -1e30).astype(np.float32)

    gw_host = np.ascontiguousarray(
        gwt.reshape(8, 8, 128, 4, 4, 128).transpose(0, 3, 2, 4, 1, 5)
        .reshape(8, 4, 128, 4096))
    uw_host = np.ascontiguousarray(
        uwt.reshape(8, 8, 128, 4, 4, 128).transpose(0, 3, 2, 4, 1, 5)
        .reshape(8, 4, 128, 4096))
    dw_host = np.ascontiguousarray(
        dwt.reshape(8, 16, 128, 1024).reshape(8, 4, 4, 128, 1024)
        .transpose(0, 1, 3, 2, 4).reshape(8, 4, 128, 4096))
    gb_host = np.ascontiguousarray(gbt.reshape(8, 16, 128).transpose(2, 0, 1).reshape(128, 128))
    ub_host = np.ascontiguousarray(ubt.reshape(8, 16, 128).transpose(2, 0, 1).reshape(128, 128))
    db_host = np.ascontiguousarray(dbt.reshape(1, 8192))
    ones_host = np.ones((1, 128), dtype=np.float32)
    cv_host = np.ascontiguousarray(np.broadcast_to(
        np.array([1e-5, 1e-6, -np.log(16.0)], np.float32), (128, 3)))
    tri_host = np.triu(np.ones((128, 128), np.float32), 1)
    pp = np.arange(128, dtype=np.uint32)[:, None]
    dum_host = np.zeros((128, 16, 2), np.uint32)
    dum_host[:, :, 0] = 60000
    dum_host[:, :, 1] = 4096 + pp
    dum_host = np.ascontiguousarray(dum_host.reshape(128, 32))
    tp_host = np.zeros((128, 4, 8, 2), np.uint32)
    for gg in range(4):
        tt = gg * 1024 + np.arange(8, dtype=np.uint32)[None, :] * 128 + pp
        tp_host[:, gg, :, 0] = tt
        tp_host[:, gg, :, 1] = tt
    tp_host = np.ascontiguousarray(tp_host.reshape(128, 64))
    idn_host = np.eye(128, dtype=np.float32)

    shared = dict(mask=band, qg=qg_host, kg=kg_host, idn=idn_host, rt=rt_host,
                  wq=wq_host, wkv=wkv_host, wo=wo_host, gw=gw_host, uw=uw_host,
                  dw=dw_host, gb=gb_host, ub=ub_host, db=db_host, ones=ones_host,
                  cv=cv_host, tri=tri_host, dum=dum_host, tp=tp_host)

    in_maps = []
    for c in range(8):
        b, q = c // 4, c % 4
        if q == 0:
            xs = np.concatenate([np.zeros((512, 1024), np.float32), x[b, 0:4096]], axis=0)
            tpos = np.concatenate([np.zeros(512), np.arange(0, 4096)])
            clip = np.full((128, 4), -1e30, np.float32)
        else:
            xs = np.ascontiguousarray(x[b, q * 4096 - 512: q * 4096 + 4096])
            tpos = np.arange(q * 4096 - 512, q * 4096 + 4096)
            clip = np.zeros((128, 4), np.float32)
        ang = tpos.astype(np.float32)[:, None] * inv[None, :]
        cs = np.concatenate([np.cos(ang), np.sin(ang)], axis=1).astype(np.float32)
        cs = cs.astype(ml_dtypes.bfloat16)
        m = dict(shared)
        m["x"] = np.ascontiguousarray(xs, dtype=np.float32)
        m["cs"] = np.ascontiguousarray(cs)
        m["clip"] = clip
        in_maps.append(m)
    return in_maps


def kernel(**inputs):
    in_maps = _host_prep(inputs)
    nc = build_nc()
    res = run_bass_kernel_spmd(nc, in_maps, core_ids=list(range(8)))
    outp = np.empty((2, 16384, 1024), dtype=np.float32)
    for c in range(8):
        b, q = c // 4, c % 4
        outp[b, q * 4096:(q + 1) * 4096] = res.results[c]["out"][0:4096]
    return outp


# version 28
# speedup vs baseline: 1.0209x; 1.0209x over previous
import numpy as np
import ml_dtypes
import concourse.bass as bass
import concourse.mybir as mybir
from concourse.bass_utils import run_bass_kernel_spmd
from concourse.bass import IndirectOffsetOnAxis

F32 = mybir.dt.float32
BF16 = mybir.dt.bfloat16
AF = mybir.ActivationFunctionType
ALU = mybir.AluOpType
AX = mybir.AxisListType

D = 1024
HD = 256
NH = 4
CT = 4096
HT = 512
NG = 4
GT = 1024
FF = 2048
NE = 8
NFC = 16
NDC = 8
KR = 6
GUR = 20
DNR = 16
CAP = 1280
SPPC = CAP // 128
LN16 = float(np.log(16.0))
G_NG = NG
G_NE = NE
G_TILES = 8
G_STAGE = 9


def build_nc():
    nc = bass.Bass("TRN2", target_bir_lowering=False)

    x_in = nc.dram_tensor("x", [HT + CT, D], F32, kind="ExternalInput").ap()
    cs_in = nc.dram_tensor("cs", [HT + CT, 256], BF16, kind="ExternalInput").ap()
    mask_in = nc.dram_tensor("mask", [128, 640], F32, kind="ExternalInput").ap()
    clip_in = nc.dram_tensor("clip", [128, 4], F32, kind="ExternalInput").ap()
    qg_in = nc.dram_tensor("qg", [128, 256], F32, kind="ExternalInput").ap()
    kg_in = nc.dram_tensor("kg", [128, 256], F32, kind="ExternalInput").ap()
    idn_in = nc.dram_tensor("idn", [128, 128], F32, kind="ExternalInput").ap()
    rt_in = nc.dram_tensor("rt", [8, 128, 8], F32, kind="ExternalInput").ap()
    wq_in = nc.dram_tensor("wq", [2, 128, 4096], F32, kind="ExternalInput").ap()
    wkv_in = nc.dram_tensor("wkv", [128, 4096], F32, kind="ExternalInput").ap()
    wo_in = nc.dram_tensor("wo", [2, 128, 4096], F32, kind="ExternalInput").ap()
    gw_in = nc.dram_tensor("gw", [NE, 4, 128, 4096], F32, kind="ExternalInput").ap()
    uw_in = nc.dram_tensor("uw", [NE, 4, 128, 4096], F32, kind="ExternalInput").ap()
    dw_in = nc.dram_tensor("dw", [NE, 4, 128, 4096], F32, kind="ExternalInput").ap()
    gb_in = nc.dram_tensor("gb", [128, 128], F32, kind="ExternalInput").ap()
    ub_in = nc.dram_tensor("ub", [128, 128], F32, kind="ExternalInput").ap()
    db_in = nc.dram_tensor("db", [1, NE * D], F32, kind="ExternalInput").ap()
    ones_in = nc.dram_tensor("ones", [1, 128], F32, kind="ExternalInput").ap()
    cv_in = nc.dram_tensor("cv", [128, 3], F32, kind="ExternalInput").ap()
    tri_in = nc.dram_tensor("tri", [128, 128], F32, kind="ExternalInput").ap()
    dum_in = nc.dram_tensor("dum", [128, 32], mybir.dt.uint32, kind="ExternalInput").ap()
    tp_in = nc.dram_tensor("tp", [128, 64], mybir.dt.uint32, kind="ExternalInput").ap()
    out = nc.dram_tensor("out", [CT + 128, D], F32, kind="ExternalOutput").ap()
    xnd_in = nc.dram_tensor("xnd", [CT, D], BF16, kind="Internal").ap()
    wdr_in = nc.dram_tensor("wdr", [CT, 10], F32, kind="Internal").ap()
    gidxd_in = nc.dram_tensor("gidxd", [NE, CAP, 2], mybir.dt.uint32, kind="Internal").ap()

    A = nc.alloc_sbuf_tensor
    maskb = A("maskb", [128, 640], BF16)
    clipv = A("clipv", [128, 4], F32)
    cflt = A("cflt", [128, 3], F32)
    qg = A("qg_sb", [128, 256], BF16)
    kg = A("kg_sb", [128, 256], BF16)
    idf = A("idf", [128, 128], F32)
    rt = A("rt_sb", [128, 64], BF16)
    gbias = A("gb_sb", [128, 128], F32)
    ubias = A("ub_sb", [128, 128], F32)
    dbias = A("db_sb", [1, 2 * D], BF16)
    ones = A("ones_sb", [1, 128], BF16)
    xnTg = A("xnTg", [128, NDC * GT], BF16)
    yg = A("yg", [128, 2 * D], F32)
    ktr = A("ktr", [128, 2 * KR * 128], BF16)
    vr = A("vr", [128, KR * 256], BF16)
    css = A("css", [128, 2 * 256], BF16)
    xnf = A("xnf", [128, D], F32)
    scrA = A("scrA", [128, D], F32)
    ctxf = A("ctxf", [128, D], F32)
    rtmp = A("rtmp", [128, 4 * 128], F32)
    kn = A("kn", [128, 256], F32)
    qTs = [A("qTs0", [128, D], BF16), A("qTs1", [128, D], BF16)]
    smb = A("smb", [128, 640], F32)
    pexp = A("pexp", [128, 640], F32)
    pTb = A("pTb", [128, 640], BF16)
    wg = A("wg", [128, 4 * 64], F32)
    sml = A("sml", [128, 40], F32)
    gur = A("gur", [128, GUR * 1024], BF16)
    dnr = A("dnr", [128, DNR * 1024], BF16)
    hbuf = A("hbuf", [128, NFC * CAP], BF16)
    sgb = A("sgb", [128, 2 * 512], BF16)
    hbuf2 = A("hbuf2", [128, 8192], BF16)
    tri = A("tri_sb", [128, 128], BF16)
    cnt8 = A("cnt8", [128, 8], BF16)
    scr8 = A("scr8", [128, 100], F32)
    slotu = A("slotu", [128, 64], mybir.dt.uint32)
    gidx_all = A("gidx_all", [128, 8 * (2 * CAP // 128)], mybir.dt.uint32)
    wgath = A("wgath", [128, 2 * 2 * SPPC], F32)
    dummy_sb = A("dummy_sb", [128, 32], mybir.dt.uint32)
    tp_sb = A("tp_sb", [128, 64], mybir.dt.uint32)
    stg_sb = A("stg_sb", [128, 8 * 512], BF16)

    C_NM, C_SSQ, C_RSTD, C_KMS, C_KRS = 0, 1, 2, 3, 4
    C_QMS, C_QRS = 8, 12
    C_NMX, C_SUME, C_RCP = 16, 20, 24
    C_P8, C_T0, C_T1 = 28, 36, 38

    PB = [nc.alloc_psum_tensor(f"pb{i}", [128, 512], F32) for i in range(8)]

    ops = {e: [] for e in ["tensor", "vector", "scalar", "gpsimd", "sync"]}
    DMA_SEMS = (["DX%d" % i for i in range(4)] + ["DW%d" % i for i in range(4)]
                + ["DS%d" % i for i in range(4)] + ["DO", "DY"])
    cnt = {s: 0 for s in ["TE", "VE", "SE", "GE"] + DMA_SEMS}
    rot = {"DX": 0, "DW": 0, "DS": 0}

    def op(eng, fn, deps=(), inc=None, n=1):
        if inc in ("DX", "DW", "DS"):
            inc = "%s%d" % (inc, rot[inc] % 4)
            rot[inc[:2]] += 1
        dd = []
        for d in deps:
            if d is None:
                continue
            s, v = d
            if v > 0:
                dd.append((s, v))
        if inc in DMA_SEMS and cnt[inc] > 0:
            dd.append((inc, cnt[inc]))
        ops[eng].append((fn, tuple(dd), inc, n))
        if inc is not None:
            cnt[inc] += n
            return (inc, cnt[inc])
        return None

    readers = {}
    wrev = {}

    def rd(res, ev):
        if ev is not None:
            readers.setdefault(res, []).append(ev)

    def wdeps(res):
        d = list(readers.get(res, ()))
        readers[res] = []
        return d

    def peek(res):
        return list(readers.get(res, ()))

    def wr(res, ev):
        wrev[res] = ev

    def rdep(res):
        e = wrev.get(res)
        return [e] if e is not None else []

    def dma(eng, dst, src, inc, deps=()):
        return op(eng, lambda E, d=dst, s=src: E.dma_start(out=d, in_=s),
                  deps=deps, inc=inc, n=16)

    D_IDF = dma("sync", idf[:], idn_in[:, :], "DX")
    D_QG = dma("gpsimd", qg[:], qg_in[:, :], "DW")
    D_KG = dma("gpsimd", kg[:], kg_in[:, :], "DW")
    D_CLIP = dma("sync", clipv[:], clip_in[:, :], "DX")
    D_GB = dma("sync", gbias[:], gb_in[:, :], "DX")
    D_UB = dma("sync", ubias[:], ub_in[:, :], "DX")
    D_CV = dma("sync", cflt[:], cv_in[:, :], "DX")

    D_MASK = dma("gpsimd", maskb[:], mask_in[:, :], "DW")
    dma("gpsimd", ones[:], ones_in[:, :], "DW")
    D_RT = op("gpsimd", lambda E: E.dma_start(
        out=rt[:].rearrange("p (dc e) -> p dc e", e=8),
        in_=rt_in.rearrange("dc p e -> p dc e")), inc="DW", n=16)
    D_TRI = dma("gpsimd", tri[:], tri_in[:, :], "DW")
    REGS = {}

    def _mkregs(E):
        for name, val in [("b511", CAP - 1), ("bct", CT - 1),
                          ("bext", CT + 127)]:
            r = E.alloc_register(name)
            E.reg_mov(r, val)
            REGS[name] = r
        return None
    op("gpsimd", _mkregs)
    EV_DUM = dma("sync", dummy_sb[:], dum_in[:, :], "DX")
    EV_TP = dma("sync", tp_sb[:], tp_in[:, :], "DX")

    gu_pos = [0]
    dn_pos = [0]

    def gu_quad_dma(src4d, dst_from_col):
        p = gu_pos[0] % (GUR // 4)
        gu_pos[0] += 1
        base = p * 4 * 1024
        deps = []
        for i in range(4):
            deps += wdeps(("gur", p * 4 + i))
        dst = dst_from_col(base)
        ev = op("gpsimd", lambda E, d=dst, s=src4d: E.dma_start(out=d, in_=s),
                deps=deps, inc="DW", n=16)
        for i in range(4):
            wr(("gur", p * 4 + i), ev)
        return [base + i * 1024 for i in range(4)], p * 4, ev

    def dn_quad_dma(src2d):
        p = dn_pos[0] % (DNR // 4)
        dn_pos[0] += 1
        base = p * 4 * 1024
        deps = []
        for i in range(4):
            deps += wdeps(("dnr", p * 4 + i))
        dst = dnr[:, base:base + 4096]
        ev = op("gpsimd", lambda E, d=dst, sa=src2d: E.dma_start(out=d, in_=sa),
                deps=deps, inc="DW", n=16)
        for i in range(4):
            wr(("dnr", p * 4 + i), ev)
        return [base + i * 1024 for i in range(4)], ev

    tctr = [0]

    def tgroup(items, src_deps, dst_ap, dst_res_list, evac_extra_deps=()):
        gidx = tctr[0]
        tctr[0] += 1
        bank = 6 + (gidx % 2)
        n = len(items)
        first = True
        tev = None
        for slot, s_ap in enumerate(items):
            pslice = PB[bank][:, slot * 128:(slot + 1) * 128]
            deps = (list(src_deps) + wdeps(("pbfree", bank)) + [D_IDF]) if first else list(src_deps)
            tev = op("tensor", lambda E, o=pslice, s=s_ap: E.transpose(o, s, idf[:]),
                     deps=deps, inc="TE" if slot == n - 1 else None)
            first = False
        wr(("pb", bank), tev)
        d1 = []
        for res in dst_res_list:
            if res is not None:
                d1 += wdeps(res)
        src_ap = PB[bank][:, 0:n * 128]
        ev = op("vector", lambda E, o=dst_ap, s=src_ap: E.tensor_copy(o, s),
                deps=[tev] + d1 + list(evac_extra_deps), inc="VE")
        rd(("pbfree", bank), ev)
        for res in dst_res_list:
            if res is not None:
                wr(res, ev)
        return ev

    def mmgroup(banks, fns, deps=()):
        d = list(deps)
        for b in set(banks):
            d += wdeps(("pbfree", b))
        ev = None
        for i, fn in enumerate(fns):
            ev = op("tensor", fn, deps=d if i == 0 else (),
                    inc="TE" if i == len(fns) - 1 else None)
        for b in set(banks):
            wr(("pb", b), ev)
        return ev

    def bank_read(b, ev):
        rd(("pbfree", b), ev)

    n_out = [0]
    init_evs = []
    wdr_evs = []
    xnd_evs = []
    stages = [stg_sb[:, k * 512:(k + 1) * 512] for k in range(8)]
    STAGE_HOME = ["stgh%d" % k for k in range(8)]
    NST = len(stages)

    WQ = {}
    WKV = {}
    WO = {}
    wevs = []
    cols, _, _ev = gu_quad_dma(wq_in[0], lambda b: gur[:, b:b + 4096])
    for dc in range(4):
        WQ[dc] = cols[dc]
    wevs.append(_ev)
    cols, _, _ev = gu_quad_dma(wq_in[1], lambda b: gur[:, b:b + 4096])
    for dc in range(4):
        WQ[4 + dc] = cols[dc]
    wevs.append(_ev)
    cols, _, _ev = gu_quad_dma(wkv_in[:, :], lambda b: gur[:, b:b + 4096])
    for dc in range(8):
        WKV[dc] = cols[dc // 2] + (dc % 2) * 512
    wevs.append(_ev)
    cols, _, _ev = gu_quad_dma(wo_in[0], lambda b: gur[:, b:b + 4096])
    for dc in range(4):
        WO[dc] = cols[dc]
    wevs.append(_ev)
    cols, _, _ev = gu_quad_dma(wo_in[1], lambda b: gur[:, b:b + 4096])
    for dc in range(4):
        WO[4 + dc] = cols[dc]
    wevs.append(_ev)
    DW_W = list(wevs)

    for g in range(G_NG):
        def early(kind, kti, WQ, WKV, WO, DW_W, g):
            halo = kind == "halo"
            j = kti - 4
            tj = j % 8
            par = kti % 2
            if halo:
                xdst, xres = scrA[:, :], "scrA"
            else:
                xdst, xres = yg[:, (tj % 2) * D:(tj % 2 + 1) * D], ("yg", tj % 2)
            xdeps = wdeps(xres)
            xev = op("sync", lambda E, d=xdst, s=x_in[kti * 128:(kti + 1) * 128, :]:
                     E.dma_start(out=d, in_=s), deps=xdeps, inc="DX", n=16)
            wr(xres, xev)
            csl = kti % 2
            cdst = css[:, csl * 256:(csl + 1) * 256]
            cev = op("sync", lambda E, d=cdst, s=cs_in[kti * 128:(kti + 1) * 128, :]:
                     E.dma_start(out=d, in_=s), deps=wdeps(("css", csl)),
                     inc="DX", n=16)
            wr(("css", csl), cev)
            cosap = css[:, csl * 256: csl * 256 + 128]
            sinap = css[:, csl * 256 + 128: csl * 256 + 256]
            xap = xdst

            e1 = op("vector", lambda E, x=xap: E.reduce_sum(
                sml[:, C_NM:C_NM + 1], x, axis=AX.X, negate=True),
                deps=[xev] + wdeps("c_nm"), inc="VE")
            rd(xres, e1)
            e2 = op("vector", lambda E: E.tensor_scalar_mul(
                sml[:, C_NM:C_NM + 1], sml[:, C_NM:C_NM + 1], 1.0 / D),
                deps=[e1], inc="VE")
            wr("c_nm", e2)
            e3 = op("scalar", lambda E, x=xap: E.activation(
                xnf[:], x, AF.Square, bias=sml[:, C_NM:C_NM + 1],
                accum_out=sml[:, C_SSQ:C_SSQ + 1]),
                deps=[e2] + wdeps("xnf"), inc="SE")
            rd(xres, e3)
            rd("c_nm", e3)
            e4 = op("scalar", lambda E: E.activation(
                sml[:, C_SSQ:C_SSQ + 1], sml[:, C_SSQ:C_SSQ + 1], AF.Ln,
                scale=1.0 / D, bias=cflt[:, 0:1]), deps=[e3, D_CV], inc="SE")
            e5 = op("scalar", lambda E: E.activation(
                sml[:, C_RSTD:C_RSTD + 1], sml[:, C_SSQ:C_SSQ + 1], AF.Exp,
                scale=-0.5), deps=[e4] + wdeps("c_rstd"), inc="SE")
            wr("c_rstd", e5)
            e6 = op("vector", lambda E, x=xap: E.tensor_scalar(
                xnf[:], x, sml[:, C_NM:C_NM + 1], sml[:, C_RSTD:C_RSTD + 1],
                op0=ALU.add, op1=ALU.mult), deps=[e5] + wdeps("xnf"), inc="VE")
            rd(xres, e6)
            rd("c_nm", e6)
            rd("c_rstd", e6)
            wr("xnf", e6)
            if not halo:
                xnb = hbuf[:, 8192 + tj * 1024: 8192 + (tj + 1) * 1024]
                blkd = wdeps(("hb", 0, tj)) if tj < 4 else wdeps(("hb", 0, tj))
                cb_ev = op("vector", lambda E, d=xnb: E.tensor_copy(d, xnf[:]),
                           deps=[e6] + blkd, inc="VE")
                rd("xnf", cb_ev)
                xd_ev = op("sync", lambda E, s=xnb,
                           d=xnd_in[g * GT + tj * 128: g * GT + (tj + 1) * 128, :]:
                           E.dma_start(out=d, in_=s), deps=[cb_ev], inc="DX", n=16)
                rd(("hb", 0, tj), xd_ev)
                wr(("hb", 0, tj), cb_ev)
                xnd_evs.append(xd_ev)

            xnT_evs = []
            xnT_aps = []
            xnTg3 = xnTg[:].rearrange("p (dc t) -> p dc t", t=GT)
            for g4 in range(2):
                items = [xnf[:, dc * 128:(dc + 1) * 128]
                         for dc in range(g4 * 4, g4 * 4 + 4)]
                if halo:
                    dst = qTs[par][:, g4 * 512:(g4 + 1) * 512]
                    dres = [("qTs", par, dc) for dc in range(g4 * 4, g4 * 4 + 4)]
                else:
                    dst = xnTg3[:, g4 * 4:(g4 + 1) * 4, tj * 128:(tj + 1) * 128]
                    dres = [("xnTg", dc, tj) for dc in range(g4 * 4, g4 * 4 + 4)]
                ev = tgroup(items, [e6], dst, dres)
                rd("xnf", ev)
                xnT_evs.append(ev)
            for dc in range(NDC):
                if halo:
                    xnT_aps.append(qTs[par][:, dc * 128:(dc + 1) * 128])
                else:
                    xnT_aps.append(xnTg[:, dc * GT + tj * 128: dc * GT + (tj + 1) * 128])

            proj_deps = xnT_evs + DW_W

            fns = []
            for dc in range(NDC):
                fns.append(lambda E, l=xnT_aps[dc], r=gur[:, WKV[dc]:WKV[dc] + 256],
                           st=(dc == 0), sp=(dc == NDC - 1):
                           E.matmul(PB[2][:, 0:256], l, r, start=st, stop=sp))
            for dc in range(NDC):
                fns.append(lambda E, l=xnT_aps[dc],
                           r=gur[:, WKV[dc] + 256:WKV[dc] + 512],
                           st=(dc == 0), sp=(dc == NDC - 1):
                           E.matmul(PB[2][:, 256:512], l, r, start=st, stop=sp))
            kv_ev = mmgroup([2], fns, deps=proj_deps)
            for dc in range(NDC):
                rd(("gur", WKV[dc] // 1024), kv_ev)

            q_ev = None
            if not halo:
                fns = []
                for nc_ in range(2):
                    for dc in range(NDC):
                        fns.append(lambda E, b=nc_, l=xnT_aps[dc],
                                   r=gur[:, WQ[dc] + nc_ * 512:WQ[dc] + nc_ * 512 + 512],
                                   st=(dc == 0), sp=(dc == NDC - 1):
                                   E.matmul(PB[b][:, :], l, r, start=st, stop=sp))
                q_ev = mmgroup([0, 1], fns, deps=proj_deps)
                for dc in range(NDC):
                    rd(("gur", WQ[dc] // 1024), q_ev)
            e7 = op("scalar", lambda E: E.activation(
                xnf[:, 0:256], PB[2][:, 0:256], AF.Square,
                accum_out=sml[:, C_KMS:C_KMS + 1]),
                deps=[kv_ev], inc="SE")
            rd("xnf", e7)
            bank_read(2, e7)
            e8 = op("scalar", lambda E: E.activation(
                sml[:, C_KRS:C_KRS + 1], sml[:, C_KMS:C_KMS + 1], AF.Ln,
                scale=1.0 / HD, bias=cflt[:, 1:2]), deps=[e7, D_CV] + wdeps("c_krs"),
                inc="SE")
            e9 = op("scalar", lambda E: E.activation(
                sml[:, C_KRS:C_KRS + 1], sml[:, C_KRS:C_KRS + 1], AF.Exp,
                scale=-0.5), deps=[e8], inc="SE")
            wr("c_krs", e9)
            e10 = op("vector", lambda E: E.scalar_tensor_tensor(
                kn[:], PB[2][:, 0:256], sml[:, C_KRS:C_KRS + 1], kg[:],
                op0=ALU.mult, op1=ALU.mult),
                deps=[e9, kv_ev, D_KG] + wdeps("kn"), inc="VE")
            rd("c_krs", e10)
            bank_read(2, e10)
            r1 = op("vector", lambda E, c=cosap: E.tensor_mul(
                rtmp[:, 0:128], kn[:, 0:128], c),
                deps=[e10, cev] + wdeps("rt0"), inc="VE")
            rd(("css", csl), r1)
            r2 = op("vector", lambda E, s=sinap: E.tensor_mul(
                rtmp[:, 128:256], kn[:, 128:256], s), deps=[r1], inc="VE")
            rd(("css", csl), r2)
            r3 = op("vector", lambda E, s=sinap: E.tensor_mul(
                rtmp[:, 256:384], kn[:, 0:128], s), deps=[r2], inc="VE")
            rd(("css", csl), r3)
            r4 = op("vector", lambda E, c=cosap: E.tensor_mul(
                rtmp[:, 384:512], kn[:, 128:256], c), deps=[r3], inc="VE")
            rd(("css", csl), r4)
            r5 = op("vector", lambda E: E.tensor_sub(
                kn[:, 0:128], rtmp[:, 0:128], rtmp[:, 128:256]), deps=[r4], inc="VE")
            r6 = op("vector", lambda E: E.tensor_add(
                kn[:, 128:256], rtmp[:, 256:384], rtmp[:, 384:512]), deps=[r5], inc="VE")
            wr("rt0", r6)
            wr("kn", r6)
            rs = kti % KR
            vev = op("vector", lambda E, d=vr[:, rs * 256:(rs + 1) * 256]:
                     E.tensor_copy(d, PB[2][:, 256:512]),
                     deps=[kv_ev] + wdeps(("vr", rs)), inc="VE")
            wr(("vr", rs), vev)
            bank_read(2, vev)

            return dict(halo=halo, j=j, tj=tj, par=par, kti=kti, csl=csl,
                        cev=cev, kv_ev=kv_ev, q_ev=q_ev, cosap=cosap,
                        sinap=sinap, WO=WO, DW_W=DW_W, r6=r6)

        def earlyB(est):
            halo = est["halo"]
            j = est["j"]
            tj = est["tj"]
            par = est["par"]
            kti = est["kti"]
            csl = est["csl"]
            cev = est["cev"]
            kv_ev = est["kv_ev"]
            q_ev = est["q_ev"]
            cosap = est["cosap"]
            sinap = est["sinap"]
            rs = kti % KR
            ktr3 = ktr[:].rearrange("p (kc r) -> p kc r", r=KR * 128)
            ev = tgroup([kn[:, 0:128], kn[:, 128:256]], [est["r6"]],
                        ktr3[:, :, rs * 128:(rs + 1) * 128],
                        [("ktr", rs, 0), ("ktr", rs, 1)])
            rd("kn", ev)
            if halo:
                return None
            eSl = []
            for h in range(NH):
                bank = h // 2
                colr = (h % 2) * 256
                eS = op("scalar", lambda E, b=bank, c=colr, h=h: E.activation(
                    xnf[:, h * 256:(h + 1) * 256], PB[b][:, c:c + 256], AF.Square,
                    accum_out=sml[:, C_QMS + h:C_QMS + h + 1]),
                    deps=[q_ev], inc="SE")
                rd("xnf", eS)
                bank_read(bank, eS)
                eSl.append(eS)
            eEl = []
            for h in range(NH):
                eL = op("scalar", lambda E, h=h: E.activation(
                    sml[:, C_QRS + h:C_QRS + h + 1],
                    sml[:, C_QMS + h:C_QMS + h + 1], AF.Ln,
                    scale=1.0 / HD, bias=cflt[:, 1:2]),
                    deps=[eSl[3], D_CV] + wdeps(("c_qrs", h)), inc="SE")
                eE = op("scalar", lambda E, h=h: E.activation(
                    sml[:, C_QRS + h:C_QRS + h + 1],
                    sml[:, C_QRS + h:C_QRS + h + 1], AF.Exp,
                    scale=-0.5, bias=cflt[:, 2:3]), deps=[eL], inc="SE")
                wr(("c_qrs", h), eE)
                eEl.append(eE)
            qstt = []
            for h in range(NH):
                bank = h // 2
                colr = (h % 2) * 256
                eT = op("vector", lambda E, b=bank, c=colr, h=h: E.scalar_tensor_tensor(
                    scrA[:, h * 256:(h + 1) * 256], PB[b][:, c:c + 256],
                    sml[:, C_QRS + h:C_QRS + h + 1], qg[:],
                    op0=ALU.mult, op1=ALU.mult),
                    deps=[eEl[3], q_ev, D_QG] + (wdeps("scrA") if h == 0 else []),
                    inc="VE")
                rd(("c_qrs", h), eT)
                bank_read(bank, eT)
                qstt.append(eT)
            qlast = None
            qe3 = scrA[:].rearrange("p (h d) -> p h d", d=256)[:, :, 0:128]
            qo3 = scrA[:].rearrange("p (h d) -> p h d", d=256)[:, :, 128:256]
            rt3 = rtmp[:].rearrange("p (h d) -> p h d", d=128)
            xf3a = xnf[:, 0:512].rearrange("p (h d) -> p h d", d=128)
            xf3b = xnf[:, 512:1024].rearrange("p (h d) -> p h d", d=128)
            cos3 = cosap.rearrange("p (o d) -> p o d", o=1).broadcast_to([128, 4, 128])
            sin3 = sinap.rearrange("p (o d) -> p o d", o=1).broadcast_to([128, 4, 128])
            t1 = op("vector", lambda E: E.tensor_mul(rt3, qe3, cos3),
                    deps=[qstt[3], cev] + wdeps("rt0"), inc="VE")
            rd(("css", csl), t1)
            t2 = op("vector", lambda E: E.tensor_mul(xf3a, qo3, sin3),
                    deps=[t1] + wdeps("xnf"), inc="VE")
            rd(("css", csl), t2)
            t3 = op("vector", lambda E: E.tensor_mul(xf3b, qe3, sin3),
                    deps=[t2], inc="VE")
            rd(("css", csl), t3)
            t4 = op("vector", lambda E: E.tensor_sub(qe3, rt3, xf3a),
                    deps=[t3], inc="VE")
            t5 = op("vector", lambda E: E.tensor_mul(rt3, qo3, cos3),
                    deps=[t4], inc="VE")
            rd(("css", csl), t5)
            qlast = op("vector", lambda E: E.tensor_add(qo3, xf3b, rt3),
                       deps=[t5], inc="VE")
            wr("rt0", qlast)
            wr("xnf", qlast)
            wr("scrA", qlast)
            qT_evs = []
            for g4 in range(2):
                items = [scrA[:, dc * 128:(dc + 1) * 128]
                         for dc in range(g4 * 4, g4 * 4 + 4)]
                dst = qTs[par][:, g4 * 512:(g4 + 1) * 512]
                dres = [("qTs", par, dc) for dc in range(g4 * 4, g4 * 4 + 4)]
                ev = tgroup(items, [qlast], dst, dres)
                rd("scrA", ev)
                qT_evs.append(ev)
            return dict(j=j, tj=tj, par=par, qT_evs=qT_evs,
                        WO=est["WO"], DW_W=est["DW_W"])

        def lateA(st):
            j = st["j"]
            tj = st["tj"]
            par = st["par"]
            qT_evs = st["qT_evs"]
            WO = st["WO"]
            DW_W = st["DW_W"]
            ctx_done = []
            hstate = {}

            def sc_blk(h):
                sa = 3 if h % 2 == 0 else 2
                fns = []
                sdeps = [qT_evs[h // 2]]
                for b in range(5):
                    krs = (j + b) % KR
                    sdeps += rdep(("ktr", krs, 0)) + rdep(("ktr", krs, 1))
                    for kc in range(2):
                        if b < 4:
                            dstp = PB[sa][:, b * 128:(b + 1) * 128]
                        else:
                            dstp = PB[4][:, h * 128:(h + 1) * 128]
                        fns.append(lambda E, d=dstp,
                                   l=qTs[par][:, (2 * h + kc) * 128:(2 * h + kc + 1) * 128],
                                   r=ktr[:, kc * KR * 128 + krs * 128:
                                         kc * KR * 128 + (krs + 1) * 128],
                                   st=(kc == 0), sp=(kc == 1):
                                   E.matmul(d, l, r, start=st, stop=sp))
                s_ev = mmgroup([sa, 4], fns, deps=sdeps)
                for b in range(5):
                    rd(("ktr", (j + b) % KR, 0), s_ev)
                    rd(("ktr", (j + b) % KR, 1), s_ev)
                rd(("qTs", par, 2 * h), s_ev)
                rd(("qTs", par, 2 * h + 1), s_ev)
                m1e = op("vector", lambda E, sa=sa: E.tensor_add(
                    smb[:, 0:512], PB[sa][:, :], maskb[:, 0:512]),
                    deps=[s_ev, D_MASK] + wdeps("smb"), inc="VE")
                bank_read(sa, m1e)
                m2e = op("vector", lambda E, h=h: E.tensor_add(
                    smb[:, 512:640], PB[4][:, h * 128:(h + 1) * 128],
                    maskb[:, 512:640]), deps=[s_ev], inc="VE")
                bank_read(4, m2e)
                smdone = m2e
                if j < 4:
                    wclip = 512 - 128 * j
                    smdone = op("vector", lambda E, w=wclip, jj=j: E.tensor_scalar_add(
                        smb[:, 0:w], smb[:, 0:w], clipv[:, jj:jj + 1]),
                        deps=[m2e, D_CLIP], inc="VE")
                wr("smb", smdone)
                nx = op("vector", lambda E, h=h: E.reduce_max(
                    sml[:, C_NMX + h:C_NMX + h + 1], smb[:, :], axis=AX.X,
                    negate=True), deps=[smdone] + wdeps(("c_nmx", h)), inc="VE")
                rd("smb", nx)
                hstate[h] = nx

            def ex_blk(h):
                nx = hstate[h]
                ex = op("scalar", lambda E, h=h: E.activation(
                    pexp[:, :], smb[:, :], AF.Exp,
                    bias=sml[:, C_NMX + h:C_NMX + h + 1],
                    accum_out=sml[:, C_SUME + h:C_SUME + h + 1]),
                    deps=[nx] + wdeps("pexp") + wdeps(("c_sume", h)), inc="SE")
                rd("smb", ex)
                rd(("c_nmx", h), ex)
                wr("pexp", ex)
                wr(("c_sume", h), ex)
                rch = op("vector", lambda E, h=h: E.reciprocal(
                    sml[:, C_RCP + h:C_RCP + h + 1],
                    sml[:, C_SUME + h:C_SUME + h + 1]), deps=[ex], inc="VE")
                rd(("c_sume", h), rch)
                hstate[h] = (ex, rch)

            def pt_blk(h):
                ex, rch = hstate[h]
                ev1 = tgroup([pexp[:, b * 128:(b + 1) * 128] for b in range(4)],
                             [ex], pTb[:, 0:512],
                             [("pTb", b) for b in range(4)])
                rd("pexp", ev1)
                ev2 = tgroup([pexp[:, 512:640]], [ex], pTb[:, 512:640],
                             [("pTb", 4)])
                rd("pexp", ev2)
                pT_evs = [ev1, ev2]
                cc = (h % 2) * 256
                fns = []
                avdeps = [ev1, ev2]
                for b in range(5):
                    krs = (j + b) % KR
                    avdeps += rdep(("vr", krs))
                    fns.append(lambda E, bb=b, d=PB[5][:, cc:cc + 256],
                               l=pTb[:, b * 128:(b + 1) * 128],
                               r=vr[:, krs * 256:(krs + 1) * 256]:
                               E.matmul(d, l, r, start=(bb == 0), stop=(bb == 4)))
                av_ev = mmgroup([5], fns, deps=avdeps)
                for b in range(5):
                    rd(("vr", (j + b) % KR), av_ev)
                    rd(("pTb", b), av_ev)
                cdone = op("vector", lambda E, h=h, cc=cc: E.tensor_scalar_mul(
                    ctxf[:, h * 256:(h + 1) * 256], PB[5][:, cc:cc + 256],
                    sml[:, C_RCP + h:C_RCP + h + 1]),
                    deps=[av_ev, rch] + rdep(("pb", 5)) +
                         (wdeps("ctxf") if h == 0 else []), inc="VE")
                bank_read(5, cdone)
                ctx_done.append(cdone)

            sc_blk(0)
            ex_blk(0)
            sc_blk(1)
            pt_blk(0)
            ex_blk(1)
            sc_blk(2)
            pt_blk(1)
            ex_blk(2)
            sc_blk(3)
            pt_blk(2)
            ex_blk(3)
            pt_blk(3)
            wr("ctxf", ctx_done[-1])
            return dict(j=j, tj=tj, ctx_done=ctx_done, WO=WO, DW_W=DW_W)

        def lateB(lst):
            j = lst["j"]
            tj = lst["tj"]
            ctx_done = lst["ctx_done"]
            WO = lst["WO"]
            DW_W = lst["DW_W"]

            ao_ev = None
            for rnd in range(2):
                ev = tgroup([ctxf[:, (rnd * 4 + i) * 128:(rnd * 4 + i + 1) * 128]
                             for i in range(4)],
                            [ctx_done[-1]], pTb[:, 0:512],
                            [("pTb", i) for i in range(4)])
                rd("ctxf", ev)
                cT_evs = [ev]
                fns = []
                for nc_ in range(2):
                    bank = 3 if nc_ == 0 else 4
                    for i in range(4):
                        dc = rnd * 4 + i
                        fns.append(lambda E, b=bank, l=pTb[:, i * 128:(i + 1) * 128],
                                   r=gur[:, WO[dc] + nc_ * 512:WO[dc] + nc_ * 512 + 512],
                                   st=(dc == 0), sp=(dc == NDC - 1):
                                   E.matmul(PB[b][:, :], l, r, start=st, stop=sp))
                if rnd == 0:
                    ao_ev = mmgroup([3, 4], fns, deps=cT_evs + DW_W)
                else:
                    for i, fn in enumerate(fns):
                        ao_ev = op("tensor", fn, deps=cT_evs if i == 0 else (),
                                   inc="TE" if i == len(fns) - 1 else None)
                    wr(("pb", 3), ao_ev)
                    wr(("pb", 4), ao_ev)
                for i in range(4):
                    rd(("pTb", i), ao_ev)
            for dc in range(NDC):
                rd(("gur", WO[dc] // 1024), ao_ev)
            ys = (tj % 2) * D
            y1 = op("vector", lambda E, ys=ys: E.tensor_add(
                yg[:, ys:ys + 512], PB[3][:, :], yg[:, ys:ys + 512]),
                deps=[ao_ev], inc="VE")
            bank_read(3, y1)
            rd(("yg", tj % 2), y1)
            y2 = op("vector", lambda E, ys=ys: E.tensor_add(
                yg[:, ys + 512:ys + D], PB[4][:, :],
                yg[:, ys + 512:ys + D]), deps=[ao_ev], inc="VE")
            bank_read(4, y2)
            rd(("yg", tj % 2), y2)
            wr(("yginit", tj), y2)

            fns = []
            for dc in range(NDC):
                fns.append(lambda E,
                           l=xnTg[:, dc * GT + tj * 128: dc * GT + (tj + 1) * 128],
                           r=rt[:, dc * 8:(dc + 1) * 8],
                           st=(dc == 0), sp=(dc == NDC - 1):
                           E.matmul(PB[3][:, 0:8], l, r, start=st, stop=sp))
            rts_d = [wrev[("xnTg", dc, tj)] for dc in range(NDC)] + [D_RT]
            r_ev = mmgroup([3], fns, deps=rts_d)
            for dc in range(NDC):
                rd(("xnTg", dc, tj), r_ev)
            P8 = sml[:, C_P8:C_P8 + 8]
            T0 = sml[:, C_T0:C_T0 + 1]
            T1 = sml[:, C_T1:C_T1 + 1]
            b1 = op("vector", lambda E: E.reduce_max(
                T0, PB[3][:, 0:8], axis=AX.X, negate=True),
                deps=[r_ev] + wdeps("rtT0"), inc="VE")
            bank_read(3, b1)
            b2 = op("scalar", lambda E: E.activation(
                P8, PB[3][:, 0:8], AF.Exp, bias=T0, accum_out=T1),
                deps=[b1] + wdeps("rtP8"), inc="SE")
            bank_read(3, b2)
            rd("rtT0", b2)
            b3 = op("vector", lambda E: E.reciprocal(T1, T1), deps=[b2], inc="VE")
            b4 = op("vector", lambda E: E.tensor_scalar_mul(P8, P8, T1),
                    deps=[b3], inc="VE")
            wr("rtP8", b4)
            b5 = op("vector", lambda E: E.reduce_max(T0, P8, axis=AX.X),
                    deps=[b4], inc="VE")
            b6 = op("vector", lambda E: E.tensor_scalar(
                pexp[:, 0:8], P8, T0, None, op0=ALU.is_ge),
                deps=[b5] + wdeps("pexp"), inc="VE")
            b7 = op("vector", lambda E: E.scalar_tensor_tensor(
                pexp[:, 8:16], pexp[:, 0:8], -1e30, P8,
                op0=ALU.mult, op1=ALU.add), deps=[b6], inc="VE")
            b8 = op("vector", lambda E: E.reduce_max(
                T0, pexp[:, 8:16], axis=AX.X), deps=[b7], inc="VE")
            b9 = op("vector", lambda E: E.tensor_scalar(
                pexp[:, 0:8], P8, T0, None, op0=ALU.is_ge), deps=[b8], inc="VE")
            b10 = op("vector", lambda E: E.tensor_mul(
                pexp[:, 0:8], pexp[:, 0:8], P8), deps=[b9], inc="VE")
            b11 = op("vector", lambda E: E.reduce_sum(
                T0, pexp[:, 0:8], axis=AX.X), deps=[b10], inc="VE")
            b12 = op("vector", lambda E: E.tensor_scalar_add(T0, T0, 1e-9),
                     deps=[b11], inc="VE")
            b13 = op("vector", lambda E: E.reciprocal(T0, T0), deps=[b12], inc="VE")
            wr("rtT0", b13)
            b14 = op("vector", lambda E, c=g * 64 + tj * 8: E.tensor_scalar_mul(
                wg[:, c:c + 8], pexp[:, 0:8], T0),
                deps=[b13] + wdeps(("wg", g * 8 + tj)), inc="VE")
            wr(("wg", g * 8 + tj), b14)
            rd("rtP8", b14)
            rd("pexp", b14)
            oi = op("sync", lambda E, ys=(tj % 2) * D, tj=tj,
                    d=out[g * GT + tj * 128: g * GT + (tj + 1) * 128, :]:
                    E.dma_start(out=d, in_=yg[:, ys:ys + D]),
                    deps=[y2], inc="DO", n=16)
            rd(("yg", tj % 2), oi)
            wr(("yginit", tj), oi)
            init_evs.append(oi)
            n_out[0] += 1
            wv_ev = op("sync", lambda E, c=g * 64 + tj * 8,
                       d=wdr_in[g * GT + tj * 128: g * GT + (tj + 1) * 128, 0:8]:
                       E.dma_start(out=d, in_=wg[:, c:c + 8]),
                       deps=[b14], inc="DX", n=16)
            rd(("wg", g * 8 + tj), wv_ev)
            wdr_evs.append(wv_ev)

        tiles = ([("halo", k) for k in range(4)] if g == 0 else []) + \
                [("own", g * 8 + j + 4) for j in range(G_TILES)]
        pend_sA = None
        pend_sB = None
        for kind, kti in tiles:
            est = early(kind, kti, WQ, WKV, WO, DW_W, g)
            if pend_sA is not None:
                pend_sB = lateA(pend_sA)
                pend_sA = None
            st = earlyB(est)
            if pend_sB is not None:
                lateB(pend_sB)
                pend_sB = None
            pend_sA = st
        if pend_sA is not None:
            lateB(lateA(pend_sA))

    NCOL = G_NG * 8
    SPP = CAP // 128
    IW = 2 * SPP
    wg3 = wg[:].rearrange("p (c e) -> p c e", e=8)
    wgdeps = [wrev[("wg", i)] for i in range(NCOL) if ("wg", i) in wrev]
    A32 = scr8[:, 0:NCOL]
    B32 = scr8[:, 32:32 + NCOL]
    C32 = scr8[:, 64:64 + NCOL]
    CNT1 = scr8[:, 96:97]
    def build_index(e):
        i1 = op("vector", lambda E, e=e: E.tensor_scalar(
            A32.rearrange("p (a b) -> p a b", b=1), wg3[:, 0:NCOL, e:e + 1], 0.0,
            None, op0=ALU.is_gt), deps=wgdeps + wdeps("scr8"), inc="VE")
        i2 = op("vector", lambda E: E.reduce_sum(
            CNT1, A32, axis=AX.X), deps=[i1], inc="VE")
        i3 = op("vector", lambda E: E.tensor_copy(
            cnt8[:].rearrange("p (a b) -> p a b", b=1),
            CNT1.rearrange("p (a b) -> p a b", b=1)
            .broadcast_to([128, 8, 1])), deps=[i2] + wdeps("cnt8"), inc="VE")
        wr("cnt8", i3)
        pse = mmgroup([0], [lambda E, e=e: E.matmul(
            PB[0][:, e * 8:(e + 1) * 8], tri[:], cnt8[:],
            start=True, stop=True)], deps=[i3, D_TRI])
        rd("cnt8", pse)
        cur, nxt = A32, B32
        lvl = 1
        sp_ = i1
        while lvl < NCOL:
            e1 = op("vector", lambda E, c=cur, n=nxt, l=lvl:
                    E.tensor_copy(n[:, 0:l], c[:, 0:l]), deps=[sp_], inc="VE")
            sp_ = op("vector", lambda E, c=cur, n=nxt, l=lvl:
                     E.tensor_add(n[:, l:NCOL], c[:, l:NCOL], c[:, 0:NCOL - l]),
                     deps=[sp_, e1], inc="VE")
            cur, nxt = nxt, (C32 if nxt is B32 else B32)
            lvl *= 2
        if cur is not B32:
            sp_ = op("vector", lambda E: E.tensor_copy(B32, C32),
                     deps=[sp_], inc="VE")
        sp_ = op("vector", lambda E: E.scalar_tensor_tensor(
            B32, B32, 1.0, A32, op0=ALU.mult, op1=ALU.subtract),
            deps=[sp_], inc="VE")
        sp_ = op("vector", lambda E, e=e: E.tensor_scalar(
            B32, B32, PB[0][:, e * 8:e * 8 + 1], None, op0=ALU.add),
            deps=[sp_, pse], inc="VE")
        bank_read(0, sp_)
        sp_ = op("vector", lambda E: E.tensor_scalar_add(C32, B32, -60000.0),
                 deps=[sp_], inc="VE")
        sp_ = op("vector", lambda E: E.tensor_mul(C32, C32, A32),
                 deps=[sp_], inc="VE")
        sp_ = op("vector", lambda E: E.tensor_scalar_add(C32, C32, 60000.0),
                 deps=[sp_], inc="VE")
        sb = (e % 2) * 32
        s10 = op("vector", lambda E, sb=sb: E.tensor_copy(
            slotu[:, sb:sb + NCOL], C32),
            deps=[sp_] + wdeps(("slotu", e % 2)), inc="VE")
        wr(("slotu", e % 2), s10)
        wr("scr8", s10)
        pf = op("sync", lambda E, e=e: E.dma_start(
            out=gidxd_in[e, 0:CAP].rearrange("(p s) j -> p (s j)", s=SPP),
            in_=dummy_sb[:, 0:IW]),
            deps=[EV_DUM] + wdeps(("gidxd", e)), inc="DX", n=16)
        scevs = []
        for c32 in range(NCOL):
            g_, tj = divmod(c32, 8)
            sc = op("gpsimd", lambda E, e=e,
                    ixs=slotu[:, sb + c32:sb + c32 + 1],
                    src_ap=tp_sb[:, g_ * 16 + tj * 2: g_ * 16 + tj * 2 + 2]:
                    E.indirect_dma_start(
                out=gidxd_in.rearrange("e s j -> (e s) j"),
                out_offset=IndirectOffsetOnAxis(ixs, 0),
                in_=src_ap,
                in_offset=None, element_offset=e * 2 * CAP,
                bounds_check=REGS["b511"], oob_is_err=False),
                deps=[pf, s10, EV_TP], inc="DW", n=16)
            rd(("slotu", e % 2), sc)
            scevs.append(sc)
        gl = op("sync", lambda E, e=e: E.dma_start(
            out=gidx_all[:, e * IW:(e + 1) * IW],
            in_=gidxd_in[e, 0:CAP].rearrange("(p s) j -> p (s j)", s=SPP)),
            deps=scevs, inc="DX", n=16)
        wr(("gidx", e), gl)
        for sc in scevs:
            rd(("gidxd", e), sc)
        rd(("gidxd", e), gl)


    for e in range(min(2, G_NE)):
        build_index(e)

    CHW = [512, 512, 256]

    def xgt_rhs(c01, dc, w):
        if c01 < 2:
            base = c01 * 4096 + dc * 512
            return hbuf2[:, base:base + w]
        return qTs[dc // 4][:, (dc % 4) * 256:(dc % 4) * 256 + w]

    def emit_gather(e, s8s, tevs_c):
        for s8 in s8s:
            c01 = s8 // 4 if s8 < 8 else 2
            sub = s8 % 4 if s8 < 8 else s8 - 8
            slot = s8 % 8
            icol = e * IW + s8 * 2
            wgo = (e % 2) * 2 * SPPC
            gw_ev = op("gpsimd", lambda E, s8=s8, wgo=wgo,
                       ix=gidx_all[:, icol:icol + 1]: E.indirect_dma_start(
                out=wgath[:, wgo + s8 * 2:wgo + (s8 + 1) * 2],
                out_offset=None,
                in_=wdr_in,
                in_offset=IndirectOffsetOnAxis(ix, 0),
                element_offset=e, bounds_check=REGS["bct"], oob_is_err=False),
                deps=rdep(("gidx", e)) + wdeps(("wgath", e % 2, s8)) + wdr_evs,
                inc="DW", n=16)
            rd(("gidx", e), gw_ev)
            wr(("wgath", e % 2, s8), gw_ev)
            xg_dst = xnTg[:, slot * 1024:(slot + 1) * 1024]
            xgdeps = wdeps(("xg", slot))
            for tj in range(8):
                xgdeps += wdeps(("xnTg", slot, tj))
            gx_ev = op("gpsimd", lambda E, d=xg_dst,
                       ix=gidx_all[:, icol:icol + 1]: E.indirect_dma_start(
                out=d, out_offset=None, in_=xnd_in,
                in_offset=IndirectOffsetOnAxis(ix, 0),
                bounds_check=REGS["bct"], oob_is_err=False),
                deps=rdep(("gidx", e)) + xgdeps + xnd_evs,
                inc="DW", n=16)
            rd(("gidx", e), gx_ev)
            wr(("xg", slot), gx_ev)
            if c01 < 2:
                h3 = hbuf2[:].rearrange("p (dc w) -> p dc w", w=512)
                t_out = h3[:, c01 * 8:c01 * 8 + 8, sub * 128:(sub + 1) * 128]
                t_in = xnTg[:, slot * 1024:(slot + 1) * 1024].rearrange(
                    "p (dc w) -> p dc w", w=128)
                tdeps = [gx_ev] + wdeps(("xgT", c01))
                tev = op("sync", lambda E, o=t_out, i=t_in:
                         E.dma_start_transpose(out=o, in_=i),
                         deps=tdeps, inc="DX", n=16)
                rd(("xg", slot), tev)
                wr(("xgT", c01), tev)
                tevs_c[c01].append(tev)
            else:
                for dc in range(NDC):
                    base = (dc % 4) * 256 + sub * 128
                    t_out = qTs[dc // 4][:, base:base + 128]
                    t_in = xnTg[:, slot * 1024 + dc * 128:
                                slot * 1024 + (dc + 1) * 128]
                    tdeps = [gx_ev] + wdeps(("xgT", c01))
                    tdeps += wdeps(("qTs", dc // 4, (dc % 4) * 2 + sub))
                    tev = op("sync", lambda E, o=t_out, i=t_in:
                             E.dma_start_transpose(out=o, in_=i),
                             deps=tdeps, inc="DX", n=16)
                    rd(("xg", slot), tev)
                    wr(("xgT", c01), tev)
                    wr(("qTs", dc // 4, (dc % 4) * 2 + sub), tev)
                    tevs_c[c01].append(tev)

    def gu_pair(e, q4, gcols, ucols, gdw):
        cols, _, gqe = gu_quad_dma(gw_in[e, q4], lambda b: gur[:, b:b + 4096])
        for i in range(4):
            gcols[q4 * 4 + i] = cols[i]
        cols, _, uqe = gu_quad_dma(uw_in[e, q4], lambda b: gur[:, b:b + 4096])
        for i in range(4):
            ucols[q4 * 4 + i] = cols[i]
            gdw[q4 * 4 + i] = [gqe, uqe]

    pend_t = {}
    if G_NE > 0:
        pend_t[0] = {0: [], 1: [], 2: []}
        emit_gather(0, range(SPP), pend_t[0])
    pend_gu = {}
    for e in range(G_NE):
        dcols = {}
        dnevs = []
        for q4 in range(4):
            cols, dqe = dn_quad_dma(dw_in[e, q4])
            for i in range(4):
                dcols[q4 * 4 + i] = cols[i]
            dnevs.append(dqe)

        dsl = (e % 2) * D
        db_ev = op("gpsimd", lambda E, d=dbias[0:1, dsl:dsl + D],
                   s=db_in[0:1, e * D:(e + 1) * D]: E.dma_start(out=d, in_=s),
                   deps=wdeps(("dbs", e % 2)), inc="DW", n=16)
        wr(("dbs", e % 2), db_ev)

        if e in pend_gu:
            gcols, ucols, gdw = pend_gu.pop(e)
        else:
            gcols, ucols, gdw = {}, {}, {}
            gu_pair(e, 0, gcols, ucols, gdw)
        tevs_c = pend_t.pop(e)

        for u in range(NFC * 3):
            fc, c01 = divmod(u, 3)
            if u in (3, 15, 27):
                gu_pair(e, u // 12 + 1, gcols, ucols, gdw)
            if u == 24 and e + 2 < G_NE:
                build_index(e + 2)
            if u == 39 and e + 1 < G_NE:
                ngc, nuc, ngd = {}, {}, {}
                gu_pair(e + 1, 0, ngc, nuc, ngd)
                pend_gu[e + 1] = (ngc, nuc, ngd)
            W = CHW[c01]
            par = u % 2
            gbk, ubk = 4 + 2 * par, 5 + 2 * par
            fns = []
            for dc in range(NDC):
                fns.append(lambda E, b=gbk, W=W,
                           l=gur[:, gcols[fc] + dc * 128:gcols[fc] + (dc + 1) * 128],
                           r=xgt_rhs(c01, dc, W),
                           st=(dc == 0), sp=(dc == NDC - 1):
                           E.matmul(PB[b][:, 0:W], l, r, start=st, stop=sp))
            for dc in range(NDC):
                fns.append(lambda E, b=ubk, W=W,
                           l=gur[:, ucols[fc] + dc * 128:ucols[fc] + (dc + 1) * 128],
                           r=xgt_rhs(c01, dc, W),
                           st=(dc == 0), sp=(dc == NDC - 1):
                           E.matmul(PB[b][:, 0:W], l, r, start=st, stop=sp))
            gudeps = list(gdw[fc]) + (tevs_c[c01] if u < 3 else [])
            gu_ev = mmgroup([gbk, ubk], fns, deps=gudeps)
            rd(("gur", gcols[fc] // 1024), gu_ev)
            rd(("gur", ucols[fc] // 1024), gu_ev)
            rd(("xgT", c01), gu_ev)
            if c01 == 2:
                for dc in range(NDC):
                    for s in range(2):
                        rd(("qTs", dc // 4, (dc % 4) * 2 + s), gu_ev)
            sl = u % 2
            sev = op("scalar", lambda E, b=gbk, col=e * 16 + fc, s=sl, W=W:
                     E.activation(sgb[:, s * 512:s * 512 + W], PB[b][:, 0:W],
                                  AF.Silu, bias=gbias[:, col:col + 1]),
                     deps=[gu_ev, D_GB] + wdeps(("sgb", sl)), inc="SE")
            bank_read(gbk, sev)
            wr(("sgb", sl), sev)
            hev = op("vector", lambda E, b=ubk, col=e * 16 + fc, s=sl, W=W,
                     d=hbuf[:, fc * 1280 + c01 * 512: fc * 1280 + c01 * 512 + W]:
                     E.scalar_tensor_tensor(
                         d, PB[b][:, 0:W], ubias[:, col:col + 1],
                         sgb[:, s * 512:s * 512 + W],
                         op0=ALU.add, op1=ALU.mult),
                     deps=[sev, gu_ev, D_UB] + wdeps(("h", fc, c01)), inc="VE")
            bank_read(ubk, hev)
            rd(("sgb", sl), hev)
            wr(("h", fc, c01), hev)

        gchunks = [[2, 3], [4, 5], [6, 7], [8, 9], [], []]
        if e + 1 < G_NE:
            pend_t[e + 1] = {0: [], 1: [], 2: []}
            emit_gather(e + 1, [0, 1], pend_t[e + 1])
        else:
            gchunks = [[]] * 6

        SUBG = [(0, 1, 2, 3), (4, 5, 6, 7), (8, 9)]
        stg = 0
        gci = 0
        for dh in range(2):
            for sg in SUBG:
                fns = []
                sweep_deps = list(dnevs) + [db_ev]
                for fc in range(NFC):
                    for sub in sg:
                        sweep_deps += rdep(("h", fc, sub // 4 if sub < 8 else 2))
                for fc in range(NFC):
                    for i, sub in enumerate(sg):
                        fns.append(lambda E, t=i,
                                   l=hbuf[:, fc * 1280 + sub * 128:
                                          fc * 1280 + (sub + 1) * 128],
                                   r=dnr[:, dcols[fc] + dh * 512:
                                         dcols[fc] + dh * 512 + 512],
                                   st=(fc == 0):
                                   E.matmul(PB[t][:, :], l, r,
                                            start=st, stop=False))
                for i, sub in enumerate(sg):
                    fns.append(lambda E, t=i,
                               r=dbias[0:1, dsl + dh * 512:
                                       dsl + dh * 512 + 512]:
                               E.matmul(PB[t][:, :], ones[0:1, :], r,
                                        start=False, stop=True))
                dn_ev = mmgroup(list(range(len(sg))), fns, deps=sweep_deps)
                rd(("dbs", e % 2), dn_ev)
                for fc in range(NFC):
                    for sub in sg:
                        rd(("h", fc, sub // 4 if sub < 8 else 2), dn_ev)
                    rd(("dnr", dcols[fc] // 1024), dn_ev)
                for i, sub in enumerate(sg):
                    k = stg % NST
                    sbuf_st = stages[k]
                    stg += 1
                    stage_home = STAGE_HOME[k]
                    wgo = (e % 2) * 2 * SPPC
                    eev = op("vector", lambda E, t=i, s=sbuf_st, sub=sub, wgo=wgo:
                             E.tensor_scalar_mul(
                                 s, PB[t][:, :],
                                 wgath[:, wgo + sub * 2: wgo + sub * 2 + 1]),
                             deps=[dn_ev] + rdep(("wgath", e % 2, sub)) +
                                  wdeps(("stage", k)) + wdeps(stage_home),
                             inc="VE")
                    bank_read(i, eev)
                    rd(("wgath", e % 2, sub), eev)
                    wr(("stage", k), eev)
                    wr(stage_home, eev)
                    sicol = e * IW + sub * 2 + 1
                    ysc = op("gpsimd", lambda E, dh=dh, s=sbuf_st, sicol=sicol,
                             ix=gidx_all[:, sicol:sicol + 1]:
                             E.indirect_dma_start(
                                 out=out,
                                 out_offset=IndirectOffsetOnAxis(ix, 0),
                                 in_=s, in_offset=None,
                                 element_offset=dh * 512,
                                 bounds_check=REGS["bext"], oob_is_err=True,
                                 compute_op=ALU.add),
                             deps=[eev] + rdep(("gidx", e)) + init_evs,
                             inc="DY", n=16)
                    rd(("gidx", e), ysc)
                    rd(("stage", k), ysc)
                    rd(stage_home, ysc)
                if e + 1 < G_NE and gchunks[gci]:
                    emit_gather(e + 1, gchunks[gci], pend_t[e + 1])
                gci += 1

    op("scalar", lambda E: E.nop(),
       deps=[("DO", 16 * n_out[0]), ("DY", cnt["DY"])])

    from contextlib import ExitStack
    with ExitStack() as _stk:
        sems = {}
        for s in ["TE", "VE", "SE", "GE"] + DMA_SEMS:
            sems[s] = _stk.enter_context(nc.semaphore("sem_" + s))
        block = _stk.enter_context(nc.Block())

        def emitter(name):
            def go(E):
                waited = {}
                for fn, deps, inc, n in ops[name]:
                    for s, v in deps:
                        if v > waited.get(s, 0):
                            E.wait_ge(sems[s], v)
                            waited[s] = v
                    instr = fn(E)
                    if inc is not None:
                        instr.then_inc(sems[inc], n)
            return go

        block.tensor(emitter("tensor"))
        block.vector(emitter("vector"))
        block.scalar(emitter("scalar"))
        block.gpsimd(emitter("gpsimd"))
        block.sync(emitter("sync"))
    return nc


def _host_prep(inputs):
    x = np.asarray(inputs["L0_attn_x"])
    Wq = np.asarray(inputs["L0_attn_Wq"])
    Wk = np.asarray(inputs["L0_attn_Wk"])
    Wv = np.asarray(inputs["L0_attn_Wv"])
    Wo = np.asarray(inputs["L0_attn_Wo"])
    qgam = np.asarray(inputs["L0_attn_q_gamma"])
    kgam = np.asarray(inputs["L0_attn_k_gamma"])
    rw = np.asarray(inputs["L0_ffn_router_w"])
    gwt = np.asarray(inputs["L0_ffn_gate_w"])
    gbt = np.asarray(inputs["L0_ffn_gate_b"])
    uwt = np.asarray(inputs["L0_ffn_up_w"])
    ubt = np.asarray(inputs["L0_ffn_up_b"])
    dwt = np.asarray(inputs["L0_ffn_down_w"])
    dbt = np.asarray(inputs["L0_ffn_down_b"])

    perm = np.concatenate([np.arange(0, 256, 2), np.arange(1, 256, 2)])
    wq_host = np.ascontiguousarray(
        Wq.reshape(4, 256, 1024)[:, perm, :].reshape(1024, 1024).T
        .reshape(2, 4, 128, 1024).transpose(0, 2, 1, 3).reshape(2, 128, 4096))
    WkT = Wk[perm, :].T
    WvT = Wv.T
    wkv_host = np.ascontiguousarray(np.concatenate(
        [WkT.reshape(8, 128, 256), WvT.reshape(8, 128, 256)], axis=2)
        .transpose(1, 0, 2).reshape(128, 4096))
    wo_host = np.ascontiguousarray(
        Wo.T.reshape(2, 4, 128, 1024).transpose(0, 2, 1, 3).reshape(2, 128, 4096))
    rt_host = np.ascontiguousarray(rw.T.reshape(8, 128, 8))
    qg_host = np.ascontiguousarray(np.broadcast_to(qgam[perm], (128, 256))).astype(np.float32)
    kg_host = np.ascontiguousarray(np.broadcast_to(kgam[perm], (128, 256))).astype(np.float32)

    inv = (1.0 / (np.float32(1e6) ** (np.arange(0, 256, 2, dtype=np.float32) /
                                      np.float32(256)))).astype(np.float32)
    qi = np.arange(128)[:, None]
    jc = np.arange(640)[None, :]
    band = np.where((jc >= qi) & (jc <= qi + 512), 0.0, -1e30).astype(np.float32)

    gw_host = np.ascontiguousarray(
        gwt.reshape(8, 8, 128, 4, 4, 128).transpose(0, 3, 2, 4, 1, 5)
        .reshape(8, 4, 128, 4096))
    uw_host = np.ascontiguousarray(
        uwt.reshape(8, 8, 128, 4, 4, 128).transpose(0, 3, 2, 4, 1, 5)
        .reshape(8, 4, 128, 4096))
    dw_host = np.ascontiguousarray(
        dwt.reshape(8, 16, 128, 1024).reshape(8, 4, 4, 128, 1024)
        .transpose(0, 1, 3, 2, 4).reshape(8, 4, 128, 4096))
    gb_host = np.ascontiguousarray(gbt.reshape(8, 16, 128).transpose(2, 0, 1).reshape(128, 128))
    ub_host = np.ascontiguousarray(ubt.reshape(8, 16, 128).transpose(2, 0, 1).reshape(128, 128))
    db_host = np.ascontiguousarray(dbt.reshape(1, 8192))
    ones_host = np.ones((1, 128), dtype=np.float32)
    cv_host = np.ascontiguousarray(np.broadcast_to(
        np.array([1e-5, 1e-6, -np.log(16.0)], np.float32), (128, 3)))
    tri_host = np.triu(np.ones((128, 128), np.float32), 1)
    pp = np.arange(128, dtype=np.uint32)[:, None]
    dum_host = np.zeros((128, 16, 2), np.uint32)
    dum_host[:, :, 0] = 60000
    dum_host[:, :, 1] = 4096 + pp
    dum_host = np.ascontiguousarray(dum_host.reshape(128, 32))
    tp_host = np.zeros((128, 4, 8, 2), np.uint32)
    for gg in range(4):
        tt = gg * 1024 + np.arange(8, dtype=np.uint32)[None, :] * 128 + pp
        tp_host[:, gg, :, 0] = tt
        tp_host[:, gg, :, 1] = tt
    tp_host = np.ascontiguousarray(tp_host.reshape(128, 64))
    idn_host = np.eye(128, dtype=np.float32)

    shared = dict(mask=band, qg=qg_host, kg=kg_host, idn=idn_host, rt=rt_host,
                  wq=wq_host, wkv=wkv_host, wo=wo_host, gw=gw_host, uw=uw_host,
                  dw=dw_host, gb=gb_host, ub=ub_host, db=db_host, ones=ones_host,
                  cv=cv_host, tri=tri_host, dum=dum_host, tp=tp_host)

    in_maps = []
    for c in range(8):
        b, q = c // 4, c % 4
        if q == 0:
            xs = np.concatenate([np.zeros((512, 1024), np.float32), x[b, 0:4096]], axis=0)
            tpos = np.concatenate([np.zeros(512), np.arange(0, 4096)])
            clip = np.full((128, 4), -1e30, np.float32)
        else:
            xs = np.ascontiguousarray(x[b, q * 4096 - 512: q * 4096 + 4096])
            tpos = np.arange(q * 4096 - 512, q * 4096 + 4096)
            clip = np.zeros((128, 4), np.float32)
        ang = tpos.astype(np.float32)[:, None] * inv[None, :]
        cs = np.concatenate([np.cos(ang), np.sin(ang)], axis=1).astype(np.float32)
        cs = cs.astype(ml_dtypes.bfloat16)
        m = dict(shared)
        m["x"] = np.ascontiguousarray(xs, dtype=np.float32)
        m["cs"] = np.ascontiguousarray(cs)
        m["clip"] = clip
        in_maps.append(m)
    return in_maps


def kernel(**inputs):
    in_maps = _host_prep(inputs)
    nc = build_nc()
    res = run_bass_kernel_spmd(nc, in_maps, core_ids=list(range(8)))
    outp = np.empty((2, 16384, 1024), dtype=np.float32)
    for c in range(8):
        b, q = c // 4, c % 4
        outp[b, q * 4096:(q + 1) * 4096] = res.results[c]["out"][0:4096]
    return outp
